# version 29
# baseline (speedup 1.0000x reference)
import numpy as np
import concourse.bass as bass
import concourse.mybir as mybir
from concourse.bass_utils import run_bass_kernel_spmd

F32 = mybir.dt.float32
BF16 = mybir.dt.bfloat16
I8 = mybir.dt.int8
AF = mybir.ActivationFunctionType
AL = mybir.AluOpType
AX = mybir.AxisListType

NCORE = 8
TOK = 2048
NT = 16
D = 1024
H = 16
HD = 64
NDB = 8
RND = 1.5 * 2.0 ** 23

M_ST1, M_ST2, M_MX, M_MN = 0, 16, 32, 48
M_M3, M_VAR3, M_R3, M_MB3, M_RQ, M_MBQ = 64, 80, 96, 112, 128, 144
M_SING = 160
M_PAIRSC = 176
M_RCOL = 184
M_INVB = 188
M_NW = 192
M_HEAD = 200
M_WRS = 264
M_FWS = 776
M_XSROW = 1800
M_GMAX = 1930
M_SDROW = 2048
M_FB = 2176


def build_nc(trunc=None, debug=False):
    nc = bass.Bass("TRN2", target_bir_lowering=False,
                   detect_race_conditions=False,
                   dynamic_dma_scratch_size=512)

    dx = nc.dram_tensor("x", [TOK, D], F32, kind="ExternalInput")
    dwq = nc.dram_tensor("wqt", [D, D], F32, kind="ExternalInput")
    dwk = nc.dram_tensor("wkt", [D, D], F32, kind="ExternalInput")
    dwv = nc.dram_tensor("wvt", [D, D], F32, kind="ExternalInput")
    dwo = nc.dram_tensor("wot", [D, D], F32, kind="ExternalInput")
    dwf = nc.dram_tensor("wft", [D, H], F32, kind="ExternalInput")
    ddel = nc.dram_tensor("delta", [H, 1], F32, kind="ExternalInput")
    dnw = nc.dram_tensor("nw", [128, NDB], F32, kind="ExternalInput")
    dfw = nc.dram_tensor("fw", [D, D], I8, kind="ExternalInput")
    dfws = nc.dram_tensor("fws", [1, D], F32, kind="ExternalInput")
    dfb = nc.dram_tensor("fb", [1, D], F32, kind="ExternalInput")
    dsm = nc.dram_tensor("segmask", [H, NCORE], F32, kind="ExternalInput")
    dout = nc.dram_tensor("out", [TOK, D], F32, kind="ExternalOutput")
    agi = nc.dram_tensor("agi", [128, 520], F32)
    ago = nc.dram_tensor("ago", [128 * NCORE, 520], F32)
    agi2 = nc.dram_tensor("agi2", [1, 8], F32)
    ago2 = nc.dram_tensor("ago2", [NCORE, 8], F32)
    if debug:
        dbg_tb = nc.dram_tensor("dbg_tb", [128, 512], F32, kind="ExternalOutput")
        dbg_misc = nc.dram_tensor("dbg_misc", [128, 3200], F32, kind="ExternalOutput")
        dbg_xres = nc.dram_tensor("dbg_xres", [128, NT * D], F32, kind="ExternalOutput")
        dbg_ktm = nc.dram_tensor("dbg_ktm", [128, NT * D], F32, kind="ExternalOutput")
        dbg_gb = nc.dram_tensor("dbg_gb", [128, 520], F32, kind="ExternalOutput")
        dbg_tb2 = nc.dram_tensor("dbg_tb2", [128, 512], F32, kind="ExternalOutput")

    bfnp = mybir.dt.np(BF16)
    c_i128b = nc.inline_tensor(np.eye(128, dtype=np.float32).astype(bfnp), name="c_i128b")
    c_i128f = nc.inline_tensor(np.eye(128, dtype=np.float32), name="c_i128f")
    c_i16f = nc.inline_tensor(np.eye(16, dtype=np.float32), name="c_i16f")
    mk = (np.arange(128)[None, :] >= np.arange(128)[:, None])
    c_mask = nc.inline_tensor(mk.astype(bfnp), name="c_mask")
    c_onescol = nc.inline_tensor(np.ones((128, 1), np.float32).astype(bfnp), name="c_onescol")
    c_onesrow = nc.inline_tensor(np.ones((1, 128), np.float32), name="c_onesrow")
    p2 = np.zeros((2, 128), np.float32)
    p2[0, 0:64] = 1.0
    p2[1, 64:128] = 1.0
    c_pat2 = nc.inline_tensor(p2, name="c_pat2")
    p16 = np.zeros((16, 128), np.float32)
    for h in range(16):
        p16[h, 64 * (h % 2): 64 * (h % 2) + 64] = 1.0
    c_pat16 = nc.inline_tensor(p16, name="c_pat16")
    pfm = np.zeros((16, 512), np.float32)
    for h in range(16):
        pfm[h, 64 * (h // 2): 64 * (h // 2) + 64] = 1.0
    c_patf = nc.inline_tensor(pfm, name="c_patf")
    pf8 = np.zeros((16, 8), np.float32)
    for h in range(16):
        pf8[h, h // 2] = 1.0
    c_patf8 = nc.inline_tensor(pf8, name="c_patf8")
    c_vals = nc.inline_tensor(np.tile(np.array([[1e-5, 1e-6, RND, 0.0]], np.float32), (128, 1)), name="c_vals")
    i16p = np.zeros((128, 16), np.float32)
    i16p[0:16, 0:16] = np.eye(16)
    c_i16pad = nc.inline_tensor(i16p, name="c_i16pad")
    orp = np.zeros((128, 128), np.float32)
    orp[0, :] = 1.0
    c_onesrowpad = nc.inline_tensor(orp, name="c_onesrowpad")
    e0 = np.zeros((128, 1), np.float32)
    e0[0, 0] = 1.0
    c_e0col = nc.inline_tensor(e0, name="c_e0col")
    p16p = np.zeros((128, 128), np.float32)
    p16p[0:16, :] = p16
    c_pat16pad = nc.inline_tensor(p16p, name="c_pat16pad")

    xres = nc.alloc_sbuf_tensor("xres", [128, NT * D], BF16)
    xnfm = nc.alloc_sbuf_tensor("xnfm", [128, NT * D], BF16)
    ktm = nc.alloc_sbuf_tensor("ktm", [128, NT * D], BF16)
    vtm = nc.alloc_sbuf_tensor("vtm", [128, NT * D], BF16)
    wA = nc.alloc_sbuf_tensor("wA", [128, NDB * D], BF16)
    wB = nc.alloc_sbuf_tensor("wB", [128, NDB * D], BF16)
    wstg = nc.alloc_sbuf_tensor("wstg", [128, D], F32)
    fstg = nc.alloc_sbuf_tensor("fstg", [128, D], I8)
    fpk = nc.alloc_sbuf_tensor("fpk", [128, 2048], F32)
    expl = nc.alloc_sbuf_tensor("expl", [128, 512], F32)
    ztm = nc.alloc_sbuf_tensor("ztm", [128, 256], F32)
    qsil = nc.alloc_sbuf_tensor("qsil", [128, D], BF16)
    qfA = nc.alloc_sbuf_tensor("qfA", [128, D], BF16)
    qfB = nc.alloc_sbuf_tensor("qfB", [128, D], BF16)
    kf = nc.alloc_sbuf_tensor("kf", [128, D], BF16)
    att = nc.alloc_sbuf_tensor("att", [128, 2048], BF16)
    osc = nc.alloc_sbuf_tensor("osc", [128, D], BF16)
    atn = nc.alloc_sbuf_tensor("atn", [128, D], F32)
    wfb = nc.alloc_sbuf_tensor("wfb", [128, 128], BF16)
    sprime = nc.alloc_sbuf_tensor("sprime", [128, 512], BF16)
    tbuf = nc.alloc_sbuf_tensor("tbuf", [128, 512], F32)
    ttmp = nc.alloc_sbuf_tensor("ttmp", [128, 512], F32)
    gb = nc.alloc_sbuf_tensor("gb", [128, 520], F32)
    sbig = nc.alloc_sbuf_tensor("sbig", [128, D], BF16)
    bbig = nc.alloc_sbuf_tensor("bbig", [128, D], BF16)
    misc = nc.alloc_sbuf_tensor("misc", [128, 3200], F32)
    si128b = nc.alloc_sbuf_tensor("si128b", [128, 128], BF16)
    si128f = nc.alloc_sbuf_tensor("si128f", [128, 128], F32)
    si16f = nc.alloc_sbuf_tensor("si16f", [16, 16], F32)
    smask = nc.alloc_sbuf_tensor("smask", [128, 128], BF16)
    sonec = nc.alloc_sbuf_tensor("sonec", [128, 1], BF16)
    spatf = nc.alloc_sbuf_tensor("spatf", [16, 512], F32)
    spatf8 = nc.alloc_sbuf_tensor("spatf8", [16, 8], F32)
    si16p = nc.alloc_sbuf_tensor("si16p", [128, 16], F32)
    sorp = nc.alloc_sbuf_tensor("sorp", [128, 128], F32)
    se0c = nc.alloc_sbuf_tensor("se0c", [128, 1], F32)
    spat16p = nc.alloc_sbuf_tensor("spat16p", [128, 128], F32)
    svals = nc.alloc_sbuf_tensor("svals", [128, 4], F32)
    nc.const_aps.register(F32, 1e-5, svals[:, 0:1])
    nc.const_aps.register(F32, 1e-6, svals[:, 1:2])
    nc.const_aps.register(F32, RND, svals[:, 2:3])

    ps_s = nc.alloc_psum_tensor("ps_s", [128, 512], F32)
    ps_a = nc.alloc_psum_tensor("ps_a", [128, 1024], F32)
    ps_b = nc.alloc_psum_tensor("ps_b", [128, 1024], F32)
    ps_t = nc.alloc_psum_tensor("ps_t", [128, 512], F32)
    ps_m = nc.alloc_psum_tensor("ps_m", [128, 512], F32)
    ps_t2 = nc.alloc_psum_tensor("ps_t2", [128, 512], F32)

    plan = []
    ckpt = {}
    sems = {}
    ctr = {}
    for name in ("ds", "da", "cc", "te", "ve", "ac", "gp"):
        sems[name] = nc.alloc_semaphore(name)
        ctr[name] = 0

    def OP(eng, meth, *a, inc=(), **k):
        incs = []
        for s, n in inc:
            ctr[s] += n
            incs.append((s, n))
        plan.append((eng, "op", meth, a, k, incs))

    def W(eng, sem, val=None):
        plan.append((eng, "w", sem, val if val is not None else ctr[sem], None, None))

    def VDR(inc=False):
        OP("vector", "drain", inc=((("ve", 1),) if inc else ()))
        return ctr["ve"]

    def ADR(inc=False):
        OP("scalar", "drain", inc=((("ac", 1),) if inc else ()))
        return ctr["ac"]

    def GDR(inc=False):
        OP("gpsimd", "drain", inc=((("gp", 1),) if inc else ()))
        return ctr["gp"]

    def mark(sem):
        return ctr[sem]

    OP("vector", "memset", misc[:, :], 0.0)
    OP("vector", "memset", fpk[:, :], 0.0)
    OP("vector", "memset", qfA[64:128, :], 0.0)
    OP("vector", "memset", qfB[0:64, :], 0.0)
    m_zero = VDR(True)
    W("sync", "ve", m_zero)
    for dst, src in ((si128b, c_i128b), (si128f, c_i128f), (si16f, c_i16f),
                     (smask, c_mask), (sonec, c_onescol),
                     (spatf, c_patf),
                     (spatf8, c_patf8), (svals, c_vals), (si16p, c_i16pad),
                     (sorp, c_onesrowpad), (se0c, c_e0col), (spat16p, c_pat16pad)):
        OP("sync", "dma_start", dst[:, :], src.ap(), inc=(("ds", 16),))
    OP("sync", "dma_start", misc[0:16, M_HEAD + 35:M_HEAD + 36], ddel.ap(), inc=(("ds", 16),), allow_slow_non_contiguous=True)
    OP("sync", "dma_start", misc[:, M_NW:M_NW + 8], dnw.ap(), inc=(("ds", 16),))
    OP("sync", "dma_start", misc[0:1, M_FWS:M_FWS + D], dfws.ap(), inc=(("ds", 16),))
    OP("sync", "dma_start", misc[0:1, M_FB:M_FB + D], dfb.ap(), inc=(("ds", 16),))
    OP("sync", "dma_start", misc[0:16, M_HEAD + 36:M_HEAD + 44], dsm.ap(), inc=(("ds", 16),))
    m_consts = mark("ds")
    ckpt['consts'] = len(plan)
    OP("vector", "memset", fpk[96:112, :], 1.0)
    OP("vector", "memset", gb[:, 512:520], 0.0)
    OP("vector", "memset", misc[0:1, M_GMAX:M_GMAX + 8], 0.0)
    m_ones16 = VDR(True)

    xcast = []
    W("scalar", "ve", m_zero)
    for t in range(NT):
        half = t % 2
        if t >= 2:
            W("scalar", "gp", xcast[t - 2])
        OP("scalar", "dma_start", fpk[:, half * 1024:(half + 1) * 1024],
           dx[t * 128:(t + 1) * 128, :], inc=(("da", 16),))
        W("gpsimd", "da", mark("da"))
        OP("gpsimd", "tensor_copy", xres[:, t * D:(t + 1) * D],
           fpk[:, half * 1024:(half + 1) * 1024])
        xcast.append(GDR(True))
    ckpt['xload'] = len(plan)

    gp_cast = []

    def load_w(dram, slot, scale_col=None, pre_wait=None):
        marks = []
        for db in range(NDB):
            if gp_cast:
                W("sync", gp_cast[-1][0], gp_cast[-1][1])
            OP("sync", "dma_start", wstg[:, :], dram[db * 128:(db + 1) * 128, :],
               inc=(("ds", 16),))
            W("vector", "ds", mark("ds"))
            if pre_wait is not None and db == 0:
                W("vector", pre_wait[0], pre_wait[1])
            if scale_col is None:
                OP("vector", "tensor_copy", slot[:, db * D:(db + 1) * D], wstg[:, :])
            else:
                OP("vector", "tensor_scalar_mul", slot[:, db * D:(db + 1) * D],
                   wstg[:, :], scale_col(db))
            m = VDR(True)
            gp_cast.append(("ve", m))
            marks.append(m)
        return marks

    ckpt['prewk'] = len(plan)
    wk_marks = load_w(dwk, wA)
    ckpt['wk'] = len(plan)
    wv_marks = load_w(dwv, wB)
    W("sync", gp_cast[-1][0], gp_cast[-1][1])
    for db in range(NDB):
        OP("sync", "dma_start", wstg[:, db * 16:(db + 1) * 16],
           dwf[db * 128:(db + 1) * 128, :], inc=(("ds", 16),))
    W("gpsimd", "ds", mark("ds"))
    OP("gpsimd", "tensor_copy", wfb[:, 0:128], wstg[:, 0:128])
    m_wf = GDR(True)
    gp_cast.append(("gp", m_wf))

    ckpt['g0'] = len(plan)
    for t in range(NT):
        W("vector", "gp", xcast[t])
        OP("vector", "tensor_reduce", misc[:, M_ST1 + t:M_ST1 + t + 1],
           xres[:, t * D:(t + 1) * D], AX.X, AL.add)
        W("scalar", "gp", xcast[t])
        OP("scalar", "activation", qsil[:, :], xres[:, t * D:(t + 1) * D], AF.Square,
           0.0, 1.0, accum_out=misc[:, M_ST2 + t:M_ST2 + t + 1])
    m_ln1s_v = VDR(True)
    m_ln1s_a = ADR(True)

    def ln_finalize(eps=1e-5):
        OP("vector", "tensor_scalar_mul", misc[:, M_M3:M_M3 + 16], misc[:, M_ST1:M_ST1 + 16], 1.0 / D)
        OP("vector", "tensor_scalar_mul", misc[:, M_VAR3:M_VAR3 + 16], misc[:, M_ST2:M_ST2 + 16], 1.0 / D)
        VDR()
        OP("vector", "scalar_tensor_tensor", misc[:, M_ST2:M_ST2 + 16], misc[:, M_M3:M_M3 + 16],
           -1.0, misc[:, M_M3:M_M3 + 16], AL.mult, AL.mult)
        VDR()
        OP("vector", "tensor_add", misc[:, M_VAR3:M_VAR3 + 16], misc[:, M_VAR3:M_VAR3 + 16],
           misc[:, M_ST2:M_ST2 + 16])
        mv = VDR(True)
        W("scalar", "ve", mv)
        OP("scalar", "activation", misc[:, M_ST1:M_ST1 + 16], misc[:, M_VAR3:M_VAR3 + 16],
           AF.Sqrt, eps, 1.0)
        ma = ADR(True)
        W("vector", "ac", ma)
        OP("vector", "reciprocal", misc[:, M_R3:M_R3 + 16], misc[:, M_ST1:M_ST1 + 16])
        VDR()
        OP("vector", "scalar_tensor_tensor", misc[:, M_MB3:M_MB3 + 16], misc[:, M_M3:M_M3 + 16],
           -1.0, misc[:, M_R3:M_R3 + 16], AL.mult, AL.mult)
        return VDR(True)

    W("vector", "ac", m_ln1s_a)
    m_ln1f = ln_finalize()

    xn_marks = []
    for t in range(NT):
        W("scalar", "ve", m_ln1f)
        if t > 0:
            W("scalar", "ve", xn_marks[-1])
        OP("scalar", "activation", kf[:, :], xres[:, t * D:(t + 1) * D], AF.Identity,
           misc[:, M_MB3 + t:M_MB3 + t + 1], misc[:, M_R3 + t:M_R3 + t + 1],
           inc=(("ac", 1),))
        m_xn = mark("ac")
        for rnd in range(2):
            W("tensor", "ac", m_xn)
            if t > 0 or rnd > 0:
                W("tensor", "ve", ctr["ve"])
            if t == 0 and rnd == 0:
                W("tensor", "ds", m_consts)
            for j in range(4):
                db = rnd * 4 + j
                OP("tensor", "matmul", ps_t[:, j * 128:(j + 1) * 128],
                   kf[:, db * 128:(db + 1) * 128], si128b[:, :],
                   start=(j == 0), stop=(j == 3),
                   inc=((("te", 1),) if j == 3 else ()))
            W("vector", "te", mark("te"))
            out3 = bass.AP(xnfm, rnd * 4 * 2048 + t * 128,
                           [[NT * D, 128], [2048, 4], [1, 128]])
            in3 = ps_t[:, :].rearrange("p (b i) -> p b i", b=4)
            OP("vector", "tensor_copy", out3, in3, inc=(("ve", 1),))
        xn_marks.append(mark("ve"))

    ckpt['ln1'] = len(plan)
    W("tensor", "gp", m_wf)
    W("tensor", "ve", xn_marks[-1])
    for t in range(NT):
        for db in range(NDB):
            last = (t == NT - 1 and db == NDB - 1)
            OP("tensor", "matmul", ps_m[:, t * 16:(t + 1) * 16],
               xnfm[:, db * 2048 + t * 128: db * 2048 + (t + 1) * 128],
               wfb[:, db * 16:(db + 1) * 16],
               start=(t == 0 and db == 0), stop=last,
               inc=((("te", 1),) if last else ()))
    m_fmm = mark("te")
    W("vector", "te", m_fmm)
    OP("vector", "tensor_copy", ztm[:, 0:256], ps_m[:, 0:256], inc=(("ve", 1),))
    m_zev = mark("ve")

    for g in range(4):
        W("tensor", "ve", m_zev if g == 0 else ctr["ve"])
        for j in range(4):
            t = g * 4 + j
            OP("tensor", "matmul", ps_t[0:16, j * 128:(j + 1) * 128],
               ztm[:, t * 16:(t + 1) * 16], si128f[:, :],
               start=(j == 0), stop=(j == 3),
               inc=((("te", 1),) if j == 3 else ()))
        W("vector", "te", mark("te"))
        OP("vector", "tensor_copy", fpk[32:48, g * 512:(g + 1) * 512],
           ps_t[0:16, 0:512], inc=(("ve", 1),))
    m_zfm = mark("ve")
    W("scalar", "ve", m_zfm)
    W("scalar", "ds", m_consts)
    OP("scalar", "activation", fpk[64:80, :], fpk[32:48, :], AF.Sigmoid,
       misc[0:16, M_HEAD + 35:M_HEAD + 36], 1.0)
    ADR()
    OP("scalar", "activation", fpk[32:48, :], fpk[64:80, :], AF.Ln, 0.0, 1.0)
    m_lsig = ADR(True)
    W("vector", "ac", m_lsig)
    W("vector", "ve", m_ones16)
    OP("vector", "tensor_tensor_scan", fpk[0:16, :], fpk[32:48, :], fpk[32:48, :],
       0.0, AL.add, AL.bypass)
    VDR()
    OP("vector", "tensor_copy", misc[0:16, M_HEAD:M_HEAD + 1], fpk[0:16, 2047:2048])
    m_lfm = VDR(True)
    W("tensor", "ve", m_lfm)
    for c in range(NT):
        OP("tensor", "matmul", ps_m[:, c * 16:(c + 1) * 16],
           fpk[:, c * 128:(c + 1) * 128], si16p[:, :],
           start=(c == 0), stop=(c == NT - 1),
           inc=((("te", 1),) if c == NT - 1 else ()))
    m_ltr = mark("te")
    W("scalar", "te", m_ltr)
    OP("scalar", "activation", expl[:, 0:256], ps_m[:, 0:256], AF.Exp, 0.0, 1.0)
    ADR()
    OP("scalar", "activation", expl[:, 256:512], ps_m[:, 0:256], AF.Exp, 0.0, -1.0)
    m_expl = ADR(True)

    ckpt['fL'] = len(plan)
    kv_ev_a = None
    kv_ev_b = None
    ktm_marks = []
    for c in range(NT):
        if kv_ev_a is not None:
            W("tensor", "ac", kv_ev_a)
            W("tensor", "ve", ktm_marks[-1])
        else:
            W("tensor", "ve", wk_marks[-1])
            W("tensor", "ve", wv_marks[-1])
        for db in range(NDB):
            lhs = xnfm[:, db * 2048 + c * 128: db * 2048 + (c + 1) * 128]
            for jh in range(2):
                last = (db == NDB - 1 and jh == 1)
                OP("tensor", "matmul", ps_a[:, jh * 512:(jh + 1) * 512], lhs,
                   wA[:, db * D + jh * 512: db * D + (jh + 1) * 512],
                   start=(db == 0), stop=(db == NDB - 1),
                   inc=((("te", 1),) if last else ()))
        m_kmm = mark("te")
        if kv_ev_b is not None:
            W("tensor", "ac", kv_ev_b)
        for db in range(NDB):
            lhs = xnfm[:, db * 2048 + c * 128: db * 2048 + (c + 1) * 128]
            for jh in range(2):
                last = (db == NDB - 1 and jh == 1)
                OP("tensor", "matmul", ps_b[:, jh * 512:(jh + 1) * 512], lhs,
                   wB[:, db * D + jh * 512: db * D + (jh + 1) * 512],
                   start=(db == 0), stop=(db == NDB - 1),
                   inc=((("te", 1),) if last else ()))
        m_vmm = mark("te")
        W("scalar", "te", m_kmm)
        if c > 0:
            W("scalar", "ve", ktm_marks[-1])
        if c == 0:
            W("scalar", "ac", m_expl)
        OP("scalar", "activation", qsil[:, :], ps_a[:, :], AF.Silu, 0.0, 1.0,
           inc=(("ac", 1),))
        kv_ev_a = mark("ac")
        W("vector", "ac", kv_ev_a)
        o4 = ktm[:, c * D:(c + 1) * D].rearrange("p (b h d) -> p b h d", b=NDB, h=2)
        i4 = qsil[:, :].rearrange("p (b h d) -> p b h d", b=NDB, h=2)
        e4 = expl[:, 256 + c * 16:256 + (c + 1) * 16].rearrange(
            "p (b h d) -> p b h d", b=NDB, h=2, d=1)
        e4b, i4b = bass.broadcast_tensor_aps(e4, i4)
        OP("vector", "tensor_tensor", o4, i4b, e4b, AL.mult, inc=(("ve", 1),))
        ktm_marks.append(mark("ve"))
        W("scalar", "te", m_vmm)
        OP("scalar", "copy", vtm[:, c * D:(c + 1) * D], ps_b[:, :],
           inc=(("ac", 1),))
        kv_ev_b = mark("ac")
        W("tensor", "ve", ktm_marks[-1])
        W("tensor", "ac", kv_ev_b)
        for p in range(NDB):
            for b in range(2):
                h = 2 * p + b
                last = (p == NDB - 1 and b == 1)
                OP("tensor", "matmul", ps_s[b * 64:(b + 1) * 64, p * 64:(p + 1) * 64],
                   ktm[:, c * D + h * 64: c * D + (h + 1) * 64],
                   vtm[:, c * D + h * 64: c * D + (h + 1) * 64],
                   start=(c == 0 and p == 0), stop=(c == NT - 1 and p == NDB - 1),
                   tile_position=(0, b * 64), skip_group_check=True,
                   inc=((("te", 1),) if last else ()))
        m_u = mark("te")
    m_sweep1 = m_u

    ckpt['sweep1'] = len(plan)
    wq_marks = load_w(dwq, wA, pre_wait=("te", m_sweep1))
    W("scalar", "ve", m_lfm)
    OP("scalar", "activation", misc[0:16, M_HEAD + 1:M_HEAD + 2],
       misc[0:16, M_HEAD:M_HEAD + 1], AF.Exp, 0.0, 1.0)
    m_lamE = ADR(True)
    W("vector", "ac", m_lamE)
    W("vector", "ds", m_consts)
    OP("vector", "tensor_scalar_mul", misc[0:16, M_HEAD + 45:M_HEAD + 53],
       spatf8[:, :], misc[0:16, M_HEAD + 1:M_HEAD + 2])
    m_pfe = VDR(True)
    W("tensor", "ve", m_pfe)
    OP("tensor", "matmul", ps_m[:, 0:8], spat16p[:, :],
       misc[:, M_HEAD + 45:M_HEAD + 53], start=True, stop=True,
       inc=(("te", 1),))
    W("vector", "te", mark("te"))
    OP("vector", "tensor_copy", misc[:, M_PAIRSC:M_PAIRSC + 8], ps_m[:, 0:8])
    m_pairsc = VDR(True)
    W("scalar", "ve", m_pairsc)
    for p in range(NDB):
        OP("scalar", "activation", gb[:, p * 64:(p + 1) * 64],
           ps_s[:, p * 64:(p + 1) * 64], AF.Identity, 0.0,
           misc[:, M_PAIRSC + p:M_PAIRSC + p + 1])
    m_bsend = ADR(True)
    W("vector", "ac", m_bsend)
    W("vector", "ve", m_ones16)
    OP("vector", "tensor_copy", gb[0:16, 512:513], misc[0:16, M_HEAD:M_HEAD + 1],
       inc=(("ve", 1),))
    m_gblam = mark("ve")
    W("sync", "ve", m_gblam)
    OP("sync", "dma_start", agi[0:128, 0:520], gb[:, 0:520], inc=(("ds", 16),))
    m_agdma = mark("ds")
    W("gpsimd", "ds", m_agdma)
    OP("gpsimd", "collective_compute", "AllGather", AL.bypass,
       replica_groups=[list(range(NCORE))], ins=[agi.ap().opt()],
       outs=[ago.ap().opt()], inc=(("cc", 1),))
    m_cc1 = mark("cc")
    ckpt['exch'] = len(plan)
    W("sync", "cc", m_cc1)
    for s in range(NCORE):
        OP("sync", "dma_start", misc[0:16, M_HEAD + 2 + s:M_HEAD + 3 + s],
           ago[s * 128:s * 128 + 16, 512:513], inc=(("ds", 16),),
           allow_slow_non_contiguous=True)
    m_lamall = mark("ds")
    q_marks = []
    ev_a = kv_ev_a
    for t in range(NT):
        if t == 0:
            W("tensor", "ve", wq_marks[-1])
        W("tensor", "ac", ev_a)
        for db in range(NDB):
            lhs = xnfm[:, db * 2048 + t * 128: db * 2048 + (t + 1) * 128]
            for jh in range(2):
                last = (db == NDB - 1 and jh == 1)
                OP("tensor", "matmul", ps_a[:, jh * 512:(jh + 1) * 512], lhs,
                   wA[:, db * D + jh * 512: db * D + (jh + 1) * 512],
                   start=(db == 0), stop=(db == NDB - 1),
                   inc=((("te", 1),) if last else ()))
        m_qmm = mark("te")
        W("scalar", "te", m_qmm)
        if q_marks:
            W("scalar", "ve", q_marks[-1])
        OP("scalar", "activation", qsil[:, :], ps_a[:, :], AF.Silu, 0.0, 1.0,
           inc=(("ac", 1),))
        ev_a = mark("ac")
        W("vector", "ac", ev_a)
        out4 = bass.AP(xnfm, t * 128, [[NT * D, 128], [2048, NDB], [64, 2], [1, 64]])
        i4 = qsil[:, :].rearrange("p (b h d) -> p b h d", b=NDB, h=2)
        e4 = expl[:, t * 16:(t + 1) * 16].rearrange("p (b h d) -> p b h d",
                                                    b=NDB, h=2, d=1)
        e4b, i4b = bass.broadcast_tensor_aps(e4, i4)
        OP("vector", "tensor_tensor", out4, i4b, e4b, AL.mult, inc=(("ve", 1),))
        q_marks.append(mark("ve"))
    m_qall = mark("ve")

    ckpt['bulkq'] = len(plan)
    W("vector", "ds", m_lamall)
    W("vector", "ve", m_ones16)
    OP("vector", "tensor_tensor_scan", misc[0:16, M_HEAD + 10:M_HEAD + 18],
       misc[0:16, M_HEAD + 2:M_HEAD + 10], misc[0:16, M_HEAD + 2:M_HEAD + 10],
       0.0, AL.add, AL.bypass)
    VDR()
    OP("vector", "tensor_mul", misc[0:16, M_HEAD + 19:M_HEAD + 27],
       misc[0:16, M_HEAD + 2:M_HEAD + 10], misc[0:16, M_HEAD + 36:M_HEAD + 44])
    VDR()
    OP("vector", "tensor_reduce", misc[0:16, M_HEAD + 18:M_HEAD + 19],
       misc[0:16, M_HEAD + 19:M_HEAD + 27], AX.X, AL.add)
    VDR()
    OP("vector", "tensor_scalar", misc[0:16, M_HEAD + 45:M_HEAD + 53],
       misc[0:16, M_HEAD + 10:M_HEAD + 18], -1.0,
       misc[0:16, M_HEAD + 18:M_HEAD + 19], AL.mult, AL.add)
    VDR()
    OP("vector", "tensor_scalar", misc[0:16, M_HEAD + 45:M_HEAD + 53],
       misc[0:16, M_HEAD + 45:M_HEAD + 53], 0.0, None, AL.min)
    m_wm1 = VDR(True)
    W("scalar", "ve", m_wm1)
    OP("scalar", "activation", misc[0:16, M_HEAD + 19:M_HEAD + 27],
       misc[0:16, M_HEAD + 45:M_HEAD + 53], AF.Exp, 0.0, 1.0)
    m_wraw = ADR(True)
    W("vector", "ac", m_wraw)
    OP("vector", "tensor_mul", misc[0:16, M_HEAD + 27:M_HEAD + 35],
       misc[0:16, M_HEAD + 19:M_HEAD + 27], misc[0:16, M_HEAD + 36:M_HEAD + 44])
    m_wmat = VDR(True)

    t_marks = []
    for s in range(NCORE):
        if t_marks:
            W("sync", "ve", t_marks[-1])
        OP("sync", "dma_start", gb[:, 0:512], ago[s * 128:(s + 1) * 128, 0:512],
           inc=(("ds", 16),))
        m_gbl = mark("ds")
        if s == 0:
            W("vector", "ve", m_wmat)
        OP("vector", "tensor_scalar_mul", misc[0:16, M_WRS:M_WRS + 512],
           spatf[:, :], misc[0:16, M_HEAD + 27 + s:M_HEAD + 28 + s])
        m_wrs = VDR(True)
        W("tensor", "ve", m_wrs)
        OP("tensor", "matmul", ps_m[:, 0:512], spat16p[:, :],
           misc[:, M_WRS:M_WRS + 512], start=True, stop=True,
           inc=(("te", 1),))
        m_wtile = mark("te")
        W("vector", "te", m_wtile)
        W("vector", "ds", m_gbl)
        if s == 0:
            OP("vector", "tensor_mul", tbuf[:, :], gb[:, 0:512], ps_m[:, 0:512])
        else:
            OP("vector", "tensor_mul", ttmp[:, :], gb[:, 0:512], ps_m[:, 0:512])
            VDR()
            OP("vector", "tensor_add", tbuf[:, :], tbuf[:, :], ttmp[:, :])
        t_marks.append(VDR(True))
    m_T = t_marks[-1]
    if debug:
        W("scalar", "ve", m_T)
        OP("scalar", "dma_start", dbg_tb[:, :], tbuf[:, :], inc=(("da", 16),))
    W("vector", "ve", m_T)
    OP("vector", "tensor_copy", sprime[:, :], tbuf[:, :], inc=(("ve", 1),))
    m_sp0 = mark("ve")

    wo_marks = load_w(dwo, wB, scale_col=lambda db: misc[:, M_NW + db:M_NW + db + 1],
                      pre_wait=("te", m_sweep1))

    ckpt['T'] = len(plan)
    m_attnev = None
    m_psbfree = None
    m_oev = None
    tr_bank_ev = [None, None]
    sp_mark = m_sp0
    m_rcol_prev = None
    res_marks = []
    for c in range(NT):
        ckpt[f's2c{c}'] = len(plan)
        trk = 0
        for which in ("q", "k"):
            for rnd in range(2):
                pst_ = ps_t if trk % 2 == 0 else ps_t2
                W("tensor", "ve", q_marks[c] if which == "q" else ktm_marks[c])
                bank_id = trk % 2
                if tr_bank_ev[bank_id] is not None:
                    W("tensor", "ac", tr_bank_ev[bank_id])
                for j in range(4):
                    db = rnd * 4 + j
                    if which == "q":
                        lhs = bass.AP(xnfm, db * 2048 + c * 128,
                                      [[NT * D, 128], [1, 128]])
                    else:
                        lhs = ktm[:, c * D + db * 128: c * D + (db + 1) * 128]
                    OP("tensor", "matmul", pst_[:, j * 128:(j + 1) * 128], lhs,
                       si128b[:, :], start=(j == 0), stop=(j == 3),
                       inc=((("te", 1),) if j == 3 else ()))
                W("scalar", "te", mark("te"))
                if which == "q":
                    OP("scalar", "copy", qfA[0:64, rnd * 512:(rnd + 1) * 512],
                       pst_[0:64, :])
                    OP("scalar", "copy", qfB[64:128, rnd * 512:(rnd + 1) * 512],
                       pst_[64:128, :], inc=(("ac", 1),))
                else:
                    OP("scalar", "copy", kf[:, rnd * 512:(rnd + 1) * 512],
                       pst_[:, :], inc=(("ac", 1),))
                tr_bank_ev[bank_id] = mark("ac")
                trk += 1
        m_tr = mark("ac")
        ckpt[f's2tr{c}'] = len(plan)
        att_marks = []
        for g in range(4):
            W("tensor", "ac", m_tr)
            if g >= 2:
                W("tensor", "ve", att_marks[g - 2])
            elif m_psbfree is not None:
                W("tensor", "ve", m_psbfree)
            half = g % 2
            for j in range(4):
                h = g * 4 + j
                p, b = h // 2, h % 2
                qsrc = qfA if b == 0 else qfB
                OP("tensor", "matmul",
                   ps_b[:, half * 512 + j * 128: half * 512 + (j + 1) * 128],
                   kf[:, p * 128:(p + 1) * 128],
                   qsrc[:, p * 128:(p + 1) * 128],
                   start=True, stop=True,
                   inc=((("te", 1),) if j == 3 else ()))
            W("vector", "te", mark("te"))
            i3 = ps_b[:, half * 512:(half + 1) * 512].rearrange("p (g i) -> p g i", g=4)
            m3 = smask[:, :].rearrange("p (g i) -> p g i", g=1)
            m3b, i3b = bass.broadcast_tensor_aps(m3, i3)
            o3 = att[:, g * 512:(g + 1) * 512].rearrange("p (g i) -> p g i", g=4)
            OP("vector", "tensor_tensor", o3, i3b, m3b, AL.mult, inc=(("ve", 1),))
            att_marks.append(mark("ve"))
        ckpt[f's2at{c}'] = len(plan)
        if m_oev is not None:
            W("tensor", "ac", m_oev)
        W("tensor", "ve", att_marks[-1])
        W("tensor", "ve", sp_mark)
        if c == 0:
            W("tensor", "ac", m_bsend)
        for p in range(NDB):
            for b in range(2):
                h = 2 * p + b
                OP("tensor", "matmul",
                   ps_a[b * 64:(b + 1) * 64, p * 128:(p + 1) * 128],
                   vtm[:, c * D + h * 64: c * D + (h + 1) * 64],
                   att[:, h * 128:(h + 1) * 128],
                   start=(p % 4 == 0), stop=False, tile_position=(0, b * 64),
                   skip_group_check=True)
            for b in range(2):
                qsrc = qfA if b == 0 else qfB
                last = (p == NDB - 1 and b == 1)
                OP("tensor", "matmul",
                   ps_a[b * 64:(b + 1) * 64, p * 128:(p + 1) * 128],
                   sprime[:, p * 64:(p + 1) * 64],
                   qsrc[:, p * 128:(p + 1) * 128],
                   start=False, stop=(p % 4 == 3), tile_position=(0, b * 64),
                   skip_group_check=True,
                   inc=((("te", 1),) if last else ()))
        m_omm = mark("te")
        if c > 0:
            W("tensor", "ve", sp_mark)
        for p in range(NDB):
            for b in range(2):
                h = 2 * p + b
                last = (p == NDB - 1 and b == 1)
                OP("tensor", "matmul", ps_s[b * 64:(b + 1) * 64, p * 64:(p + 1) * 64],
                   ktm[:, c * D + h * 64: c * D + (h + 1) * 64],
                   vtm[:, c * D + h * 64: c * D + (h + 1) * 64],
                   start=(c == 0 and p == 0), stop=(c == NT - 1 and p == NDB - 1),
                   tile_position=(0, b * 64), skip_group_check=True,
                   inc=((("te", 1),) if last else ()))
        m_umm = mark("te")
        W("vector", "te", m_umm)
        OP("vector", "tensor_add", sprime[:, :], ps_s[:, :], tbuf[:, :],
           inc=(("ve", 1),))
        sp_mark = mark("ve")
        ckpt[f's2u{c}'] = len(plan)
        W("scalar", "te", m_omm)
        OP("scalar", "copy", osc[:, :], ps_a[:, :], inc=(("ac", 1),))
        m_oev = mark("ac")
        OP("scalar", "activation", qsil[:, :], osc[:, :], AF.Square, 0.0, 1.0,
           inc=(("ac", 1),))
        m_osq = mark("ac")
        ckpt[f's2r{c}'] = len(plan)
        if c == 0:
            W("tensor", "ve", wo_marks[-1])
        W("tensor", "ve", att_marks[-1])
        if m_psbfree is not None:
            W("tensor", "ve", m_psbfree)
        W("tensor", "ac", m_oev)
        for db in range(NDB):
            for jh in range(2):
                last = (db == NDB - 1 and jh == 1)
                OP("tensor", "matmul", ps_b[:, jh * 512:(jh + 1) * 512],
                   osc[:, db * 128:(db + 1) * 128],
                   wB[:, db * D + jh * 512: db * D + (jh + 1) * 512],
                   start=(db == 0), stop=(db == NDB - 1),
                   inc=((("te", 1),) if last else ()))
        m_womm = mark("te")
        W("tensor", "ac", m_osq)
        if m_rcol_prev is not None:
            W("tensor", "ac", m_rcol_prev[0])
        for db in range(NDB):
            OP("tensor", "matmul", ps_m[:, 0:1],
               qsil[:, db * 128:(db + 1) * 128], sonec[:, :],
               start=(db == 0), stop=(db == NDB - 1),
               inc=((("te", 1),) if db == NDB - 1 else ()))
        m_ssum = mark("te")
        W("scalar", "te", m_ssum)
        OP("scalar", "activation", misc[:, M_RCOL + 1:M_RCOL + 2],
           ps_m[:, 0:1], AF.Sqrt, 1e-6, 1.0 / D, inc=(("ac", 1),))
        m_sd = mark("ac")
        m_rcol_prev = (m_sd, None)
        W("vector", "ac", m_sd)
        OP("vector", "reciprocal", misc[:, M_RCOL:M_RCOL + 1],
           misc[:, M_RCOL + 1:M_RCOL + 2])
        m_rcol = VDR(True)
        W("vector", "te", m_womm)
        if res_marks:
            W("vector", "ve", res_marks[-1])
        OP("vector", "tensor_scalar", atn[:, :], ps_b[:, :], 1.0, 0.0,
           AL.mult, AL.add, accum_out=misc[:, M_SING:M_SING + 1],
           inc=(("ve", 1),))
        m_psbfree = mark("ve")
        VDR()
        W("scalar", "te", m_ssum)
        W("scalar", "ve", m_psbfree)
        OP("scalar", "activation", qsil[:, :], atn[:, :], AF.Square, 0.0, 1.0,
           accum_out=misc[:, M_SING + 1:M_SING + 2], inc=(("ac", 1),))
        W("vector", "ac", mark("ac"))
        OP("vector", "tensor_scalar_mul", misc[:, M_SING + 2:M_SING + 3],
           misc[:, M_SING:M_SING + 1], 1.0 / D)
        OP("vector", "tensor_scalar_mul", misc[:, M_SING + 3:M_SING + 4],
           misc[:, M_SING + 1:M_SING + 2], 1.0 / D)
        VDR()
        OP("vector", "scalar_tensor_tensor", misc[:, M_SING + 4:M_SING + 5],
           misc[:, M_SING + 2:M_SING + 3], -1.0, misc[:, M_SING + 2:M_SING + 3],
           AL.mult, AL.mult)
        VDR()
        OP("vector", "tensor_add", misc[:, M_SING + 3:M_SING + 4],
           misc[:, M_SING + 3:M_SING + 4], misc[:, M_SING + 4:M_SING + 5])
        VDR()
        OP("vector", "scalar_tensor_tensor", misc[:, M_SING + 4:M_SING + 5],
           misc[:, M_RCOL:M_RCOL + 1], 1.0, misc[:, M_RCOL:M_RCOL + 1],
           AL.mult, AL.mult)
        VDR()
        OP("vector", "tensor_mul", misc[:, M_SING + 3:M_SING + 4],
           misc[:, M_SING + 3:M_SING + 4], misc[:, M_SING + 4:M_SING + 5])
        mv = VDR(True)
        W("scalar", "ve", mv)
        OP("scalar", "activation", misc[:, M_SING + 5:M_SING + 6],
           misc[:, M_SING + 3:M_SING + 4], AF.Sqrt, 1e-5, 1.0, inc=(("ac", 1),))
        W("vector", "ac", mark("ac"))
        OP("vector", "reciprocal", misc[:, M_SING + 6:M_SING + 7],
           misc[:, M_SING + 5:M_SING + 6])
        VDR()
        OP("vector", "tensor_mul", misc[:, M_SING + 6:M_SING + 7],
           misc[:, M_SING + 6:M_SING + 7], misc[:, M_RCOL:M_RCOL + 1])
        VDR()
        OP("vector", "scalar_tensor_tensor", misc[:, M_SING + 7:M_SING + 8],
           misc[:, M_SING + 2:M_SING + 3], -1.0, misc[:, M_SING + 6:M_SING + 7],
           AL.mult, AL.mult)
        VDR()
        mvv = VDR(True)
        W("scalar", "ve", mvv)
        OP("scalar", "activation", fpk[:, 0:1024], atn[:, :], AF.Identity,
           misc[:, M_SING + 7:M_SING + 8], misc[:, M_SING + 6:M_SING + 7],
           inc=(("ac", 1),))
        W("vector", "ac", mark("ac"))
        OP("vector", "engine_nop", inc=(("ve", 1),))
        m_attnev = mark("ve")
        ckpt[f's2w{c}'] = len(plan)
        OP("vector", "tensor_add", xres[:, c * D:(c + 1) * D],
           xres[:, c * D:(c + 1) * D], fpk[:, 0:1024])
        res_marks.append(VDR(True))

    ckpt['sweep2'] = len(plan)
    fw_marks = []
    for db in range(NDB):
        if fw_marks:
            W("sync", "gp", fw_marks[-1])
        OP("sync", "dma_start", fstg[:, :], dfw[db * 128:(db + 1) * 128, :],
           inc=(("ds", 16),))
        W("gpsimd", "ds", mark("ds"))
        if db == 0:
            W("gpsimd", "ve", m_qall)
        OP("gpsimd", "tensor_copy", wA[:, db * D:(db + 1) * D], fstg[:, :])
        fw_marks.append(GDR(True))

    for t in range(NT):
        OP("vector", "tensor_reduce", misc[:, M_ST1 + t:M_ST1 + t + 1],
           xres[:, t * D:(t + 1) * D], AX.X, AL.add)
        OP("vector", "tensor_reduce", misc[:, M_MX + t:M_MX + t + 1],
           xres[:, t * D:(t + 1) * D], AX.X, AL.max)
        OP("vector", "tensor_reduce", misc[:, M_MN + t:M_MN + t + 1],
           xres[:, t * D:(t + 1) * D], AX.X, AL.min)
        W("scalar", "ve", res_marks[t])
        W("scalar", "ve", m_attnev)
        OP("scalar", "activation", qsil[:, :], xres[:, t * D:(t + 1) * D], AF.Square,
           0.0, 1.0, accum_out=misc[:, M_ST2 + t:M_ST2 + t + 1])
    VDR(True)
    m2a = ADR(True)
    W("vector", "ac", m2a)
    m_ln2f = ln_finalize()
    W("vector", "ve", m_ln2f)
    OP("vector", "tensor_sub", misc[:, M_ST1:M_ST1 + 16], misc[:, M_MX:M_MX + 16],
       misc[:, M_M3:M_M3 + 16])
    OP("vector", "tensor_sub", misc[:, M_ST2:M_ST2 + 16], misc[:, M_M3:M_M3 + 16],
       misc[:, M_MN:M_MN + 16])
    VDR()
    OP("vector", "tensor_max", misc[:, M_ST1:M_ST1 + 16], misc[:, M_ST1:M_ST1 + 16],
       misc[:, M_ST2:M_ST2 + 16])
    VDR()
    OP("vector", "tensor_mul", misc[:, M_ST1:M_ST1 + 16], misc[:, M_ST1:M_ST1 + 16],
       misc[:, M_R3:M_R3 + 16])
    VDR()
    OP("vector", "tensor_reduce", misc[:, M_ST2:M_ST2 + 1], misc[:, M_ST1:M_ST1 + 16],
       AX.X, AL.max)
    m_rowabs = VDR(True)
    W("tensor", "ve", m_rowabs)
    OP("tensor", "matmul", ps_m[0:1, 256:384], misc[:, M_ST2:M_ST2 + 1],
       si128f[:, :], start=True, stop=True, inc=(("te", 1),))
    W("vector", "te", mark("te"))
    OP("vector", "tensor_reduce", misc[0:1, M_GMAX:M_GMAX + 1],
       ps_m[0:1, 256:384], AX.X, AL.max)
    m_locmax = VDR(True)
    W("sync", "ve", m_locmax)
    W("sync", "ve", m_ones16)
    OP("sync", "dma_start", agi2[0:1, 0:8], misc[0:1, M_GMAX:M_GMAX + 8],
       inc=(("ds", 16),))
    W("gpsimd", "ds", mark("ds"))
    OP("gpsimd", "collective_compute", "AllGather", AL.bypass,
       replica_groups=[list(range(NCORE))], ins=[agi2.ap().opt()],
       outs=[ago2.ap().opt()], inc=(("cc", 1),))
    m_cc2 = mark("cc")
    W("sync", "cc", m_cc2)
    OP("sync", "dma_start", misc[0:8, M_HEAD + 44:M_HEAD + 45], ago2[0:8, 0:1],
       inc=(("ds", 16),), allow_slow_non_contiguous=True)
    m_gm8 = mark("ds")
    W("tensor", "ds", m_gm8)
    OP("tensor", "matmul", ps_m[0:1, 384:392], misc[0:8, M_HEAD + 44:M_HEAD + 45],
       si16f[0:8, 0:8], start=True, stop=True, inc=(("te", 1),))
    W("vector", "te", mark("te"))
    OP("vector", "tensor_reduce", misc[0:1, M_GMAX + 1:M_GMAX + 2],
       ps_m[0:1, 384:392], AX.X, AL.max)
    VDR()
    OP("vector", "tensor_scalar", misc[0:1, M_GMAX + 2:M_GMAX + 3],
       misc[0:1, M_GMAX + 1:M_GMAX + 2], 1.0 / 127.0, 1e-5, AL.mult, AL.max)
    VDR()
    OP("vector", "reciprocal", misc[0:1, M_GMAX + 3:M_GMAX + 4],
       misc[0:1, M_GMAX + 2:M_GMAX + 3])
    m_xs = VDR(True)
    W("tensor", "ve", m_xs)
    OP("tensor", "matmul", ps_m[:, 392:393], sorp[:, :],
       misc[:, M_GMAX + 3:M_GMAX + 4], start=True, stop=True)
    OP("tensor", "matmul", ps_m[0:1, 136:264],
       misc[:, M_GMAX + 2:M_GMAX + 3], sorp[:, :], start=True, stop=True,
       inc=(("te", 1),))
    W("vector", "te", mark("te"))
    OP("vector", "tensor_copy", misc[:, M_INVB:M_INVB + 1], ps_m[:, 392:393])
    VDR()
    OP("vector", "tensor_copy", misc[0:1, M_XSROW:M_XSROW + 128], ps_m[0:1, 136:264])
    VDR()
    OP("vector", "tensor_scalar_mul", misc[:, M_RQ:M_RQ + 16],
       misc[:, M_R3:M_R3 + 16], misc[:, M_INVB:M_INVB + 1])
    OP("vector", "tensor_scalar_mul", misc[:, M_MBQ:M_MBQ + 16],
       misc[:, M_MB3:M_MB3 + 16], misc[:, M_INVB:M_INVB + 1])
    m_quantc = VDR(True)
    m_last_ev = m_quantc
    W("tensor", "ve", m_quantc)
    for jh in range(2):
        OP("tensor", "matmul", ps_t[:, 0:512],
           misc[:, M_XSROW:M_XSROW + 128],
           misc[:, M_FWS + jh * 512:M_FWS + (jh + 1) * 512],
           start=True, stop=True, inc=(("te", 1),))
        W("vector", "te", mark("te"))
        OP("vector", "tensor_copy", sbig[:, jh * 512:(jh + 1) * 512], ps_t[:, 0:512],
           inc=(("ve", 1),))
        W("tensor", "ve", mark("ve"))
    for jh in range(2):
        OP("tensor", "matmul", ps_t[:, 0:512], sorp[:, :],
           misc[:, M_FB + jh * 512:M_FB + (jh + 1) * 512],
           start=True, stop=True, inc=(("te", 1),))
        W("vector", "te", mark("te"))
        OP("vector", "tensor_copy", bbig[:, jh * 512:(jh + 1) * 512], ps_t[:, 0:512],
           inc=(("ve", 1),))
        if jh == 0:
            W("tensor", "ve", mark("ve"))
    m_sbig = mark("ve")

    m_ffnev = {}
    m_psfree = {}
    prev_out_dma = None
    prev_tr = None
    def emit_quant(t, tr_done_mark):
        OP("scalar", "activation", osc[:, :], xres[:, t * D:(t + 1) * D], AF.Identity,
           misc[:, M_MBQ + t:M_MBQ + t + 1], misc[:, M_RQ + t:M_RQ + t + 1],
           inc=(("ac", 1),))
        W("vector", "ac", mark("ac"))
        if tr_done_mark is not None:
            W("vector", "te", tr_done_mark)
        OP("vector", "tensor_scalar", kf[:, :], osc[:, :], RND, RND, AL.add,
           AL.subtract, inc=(("ve", 1),))
        return mark("ve")

    W("scalar", "ve", m_quantc)
    rnd_marks = {0: emit_quant(0, None)}
    m_trlast = None
    for t in range(NT):
        pst = ps_a if t % 2 == 0 else ps_b
        m_rnd = rnd_marks[t]
        for rnd in range(2):
            pst_ = ps_t if rnd == 0 else ps_t2
            W("tensor", "ve", m_rnd)
            if tr_bank_ev[rnd] is not None:
                W("tensor", "ac", tr_bank_ev[rnd])
            if t == 0 and rnd == 0:
                W("tensor", "ve", m_sbig)
            for j in range(4):
                db = rnd * 4 + j
                OP("tensor", "matmul", pst_[:, j * 128:(j + 1) * 128],
                   kf[:, db * 128:(db + 1) * 128], si128b[:, :],
                   start=(j == 0), stop=(j == 3),
                   inc=((("te", 1),) if j == 3 else ()))
            m_trlast = mark("te")
            W("scalar", "te", m_trlast)
            OP("scalar", "copy", qfA[:, rnd * 512:(rnd + 1) * 512], pst_[:, :],
               inc=(("ac", 1),))
            tr_bank_ev[rnd] = mark("ac")
        prev_tr = mark("ac")
        W("tensor", "ac", prev_tr)
        if t == 0:
            W("tensor", "gp", fw_marks[-1])
        if t >= 2:
            W("tensor", "ve", m_psfree[t - 2])
        for db in range(NDB):
            for jh in range(2):
                last = (db == NDB - 1 and jh == 1)
                OP("tensor", "matmul", pst[:, jh * 512:(jh + 1) * 512],
                   qfA[:, db * 128:(db + 1) * 128],
                   wA[:, db * D + jh * 512: db * D + (jh + 1) * 512],
                   start=(db == 0), stop=(db == NDB - 1),
                   inc=((("te", 1),) if last else ()))
        m_fmm2 = mark("te")
        if t + 1 < NT:
            rnd_marks[t + 1] = emit_quant(t + 1, m_trlast)
        W("vector", "te", m_fmm2)
        OP("vector", "tensor_mul", atn[:, :], pst[:, :], sbig[:, :],
           inc=(("ve", 1),))
        m_psfree[t] = mark("ve")
        VDR()
        if prev_out_dma is not None:
            W("vector", "da", prev_out_dma)
        OP("vector", "scalar_tensor_tensor", fpk[:, 0:1024], atn[:, :], 1.0,
           bbig[:, :], AL.mult, AL.add, accum_out=misc[:, M_SING:M_SING + 1])
        VDR(True)
        W("scalar", "ve", mark("ve"))
        OP("scalar", "activation", qsil[:, :], fpk[:, 0:1024], AF.Square, 0.0, 1.0,
           accum_out=misc[:, M_SING + 1:M_SING + 2], inc=(("ac", 1),))
        W("vector", "ac", mark("ac"))
        OP("vector", "tensor_scalar_mul", misc[:, M_SING + 2:M_SING + 3],
           misc[:, M_SING:M_SING + 1], 1.0 / D)
        OP("vector", "tensor_scalar_mul", misc[:, M_SING + 3:M_SING + 4],
           misc[:, M_SING + 1:M_SING + 2], 1.0 / D)
        VDR()
        OP("vector", "scalar_tensor_tensor", misc[:, M_SING + 4:M_SING + 5],
           misc[:, M_SING + 2:M_SING + 3], -1.0, misc[:, M_SING + 2:M_SING + 3],
           AL.mult, AL.mult)
        VDR()
        OP("vector", "tensor_add", misc[:, M_SING + 3:M_SING + 4],
           misc[:, M_SING + 3:M_SING + 4], misc[:, M_SING + 4:M_SING + 5])
        mv = VDR(True)
        W("scalar", "ve", mv)
        OP("scalar", "activation", misc[:, M_SING + 5:M_SING + 6],
           misc[:, M_SING + 3:M_SING + 4], AF.Sqrt, 1e-5, 1.0, inc=(("ac", 1),))
        W("vector", "ac", mark("ac"))
        OP("vector", "reciprocal", misc[:, M_SING + 6:M_SING + 7],
           misc[:, M_SING + 5:M_SING + 6])
        VDR()
        OP("vector", "scalar_tensor_tensor", misc[:, M_SING + 7:M_SING + 8],
           misc[:, M_SING + 2:M_SING + 3], -1.0, misc[:, M_SING + 6:M_SING + 7],
           AL.mult, AL.mult)
        VDR()
        mvv = VDR(True)
        W("scalar", "ve", mvv)
        OP("scalar", "activation", atn[:, :], fpk[:, 0:1024], AF.Identity,
           misc[:, M_SING + 7:M_SING + 8], misc[:, M_SING + 6:M_SING + 7],
           inc=(("ac", 1),))
        W("vector", "ac", mark("ac"))
        OP("vector", "tensor_add", fpk[:, 1024:2048], xres[:, t * D:(t + 1) * D],
           atn[:, :], inc=(("ve", 1),))
        m_ffnev[t] = mark("ve")
        W("scalar", "ve", m_ffnev[t])
        OP("scalar", "dma_start", dout[t * 128:(t + 1) * 128, :], fpk[:, 1024:2048],
           inc=(("da", 16),))
        prev_out_dma = mark("da")
    W("scalar", "da", ctr["da"])

    if debug:
        OP("scalar", "copy", atn[:, 0:512], tbuf[:, 0:512])
        ADR(True)
        OP("scalar", "dma_start", dbg_gb2 := dbg_tb2[:, :] if False else dbg_tb2[:, :], atn[:, 0:512], inc=(("da", 16),))
        OP("scalar", "dma_start", dbg_misc[:, :], misc[:, :], inc=(("da", 16),))
        OP("scalar", "dma_start", dbg_gb[:, :], gb[:, :], inc=(("da", 16),))
        for t in range(NT):
            W("scalar", "da", ctr["da"])
            OP("scalar", "copy", atn[:, :], xres[:, t * D:(t + 1) * D])
            ADR(True)
            OP("scalar", "dma_start", dbg_xres[:, t * D:(t + 1) * D], atn[:, :],
               inc=(("da", 16),))
            W("scalar", "da", ctr["da"])
            OP("scalar", "copy", atn[:, :], ktm[:, t * D:(t + 1) * D])
            ADR(True)
            OP("scalar", "dma_start", dbg_ktm[:, t * D:(t + 1) * D], atn[:, :],
               inc=(("da", 16),))
        W("scalar", "da", ctr["da"])

    if trunc is not None:
        plan[:] = plan[:ckpt[trunc]]
        tctr = {k: 0 for k in ctr}
        for it in plan:
            if it[1] == "op":
                for s, n in it[5]:
                    tctr[s] += n
        for k in ctr:
            ctr[k] = tctr[k]
        plan.append(("sync", "w", "ds", ctr["ds"], None, None))
        plan.append(("scalar", "w", "gp", ctr["gp"], None, None))
        plan.append(("scalar", "w", "ve", ctr["ve"], None, None))
        plan.append(("scalar", "w", "ac", ctr["ac"], None, None))
        plan.append(("scalar", "w", "te", ctr["te"], None, None))
        plan.append(("scalar", "w", "cc", ctr["cc"], None, None))
        plan.append(("scalar", "w", "da", ctr["da"], None, None))
        plan.append(("scalar", "op", "dma_start",
                     (dout[0:128, :], misc[:, 0:1024]), {}, [("da", 16)]))
        ctr["da"] += 16
        plan.append(("scalar", "w", "da", ctr["da"], None, None))
        for e in ("tensor", "vector", "gpsimd"):
            plan.append((e, "op", "nop", (), {}, []))

    def runner(name):
        def f(eng):
            for it in plan:
                if it[0] != name:
                    continue
                if it[1] == "w":
                    eng.wait_ge(sems[it[2]], it[3])
                else:
                    _, _, meth, a, k, incs = it
                    ins = getattr(eng, meth)(*a, **k)
                    for s, n in incs:
                        ins = ins.then_inc(sems[s], n)
        return f

    with nc.Block() as blk:
        blk.sync(runner("sync"))
        blk.tensor(runner("tensor"))
        blk.vector(runner("vector"))
        blk.scalar(runner("scalar"))
        blk.gpsimd(runner("gpsimd"))
    return nc


_cache = {}


def kernel(L0_attn_x, L0_attn_Wq, L0_attn_Wk, L0_attn_Wv, L0_attn_Wo, L0_attn_Wf,
           L0_attn_delta, L0_attn_norm_weight, L0_ffn_weight_int8,
           L0_ffn_weight_scale, L0_ffn_bias):
    x = np.ascontiguousarray(np.asarray(L0_attn_x, dtype=np.float32).reshape(16384, 1024))
    wqt = np.ascontiguousarray(np.asarray(L0_attn_Wq, dtype=np.float32).T)
    wkt = np.ascontiguousarray(np.asarray(L0_attn_Wk, dtype=np.float32).T)
    wvt = np.ascontiguousarray(np.asarray(L0_attn_Wv, dtype=np.float32).T)
    wot = np.ascontiguousarray(np.asarray(L0_attn_Wo, dtype=np.float32).T)
    wft = np.ascontiguousarray(np.asarray(L0_attn_Wf, dtype=np.float32).T)
    delta = np.ascontiguousarray(np.asarray(L0_attn_delta, dtype=np.float32).reshape(16, 1))
    nw = np.ascontiguousarray(np.asarray(L0_attn_norm_weight, dtype=np.float32).reshape(8, 128).T)
    fw = np.ascontiguousarray(np.asarray(L0_ffn_weight_int8).T)
    fws = np.ascontiguousarray(np.asarray(L0_ffn_weight_scale, dtype=np.float32).reshape(1, 1024))
    fb = np.ascontiguousarray(np.asarray(L0_ffn_bias, dtype=np.float32).reshape(1, 1024))

    if "nc" not in _cache:
        _cache["nc"] = build_nc()
    nc = _cache["nc"]
    in_maps = []
    for s in range(NCORE):
        sm = np.zeros((16, 8), dtype=np.float32)
        sm[:, :s] = 1.0
        in_maps.append({
            "x": np.ascontiguousarray(x[s * TOK:(s + 1) * TOK, :]),
            "wqt": wqt, "wkt": wkt, "wvt": wvt, "wot": wot, "wft": wft,
            "delta": delta, "nw": nw, "fw": fw, "fws": fws, "fb": fb,
            "segmask": sm,
        })
    res = run_bass_kernel_spmd(nc, in_maps, core_ids=list(range(NCORE)))
    out = np.concatenate([np.asarray(r["out"]) for r in res.results], axis=0)
    return out.reshape(1, 16384, 1024).astype(np.float32)


# version 32 (final)
# speedup vs baseline: 1.0432x; 1.0432x over previous
import numpy as np
import concourse.bass as bass
import concourse.mybir as mybir
from concourse.bass_utils import run_bass_kernel_spmd

F32 = mybir.dt.float32
BF16 = mybir.dt.bfloat16
I8 = mybir.dt.int8
AF = mybir.ActivationFunctionType
AL = mybir.AluOpType
AX = mybir.AxisListType

NCORE = 8
TOK = 2048
NT = 16
D = 1024
H = 16
HD = 64
NDB = 8
RND = 1.5 * 2.0 ** 23

M_ST1, M_ST2, M_MX, M_MN = 0, 16, 32, 48
M_M3, M_VAR3, M_R3, M_MB3, M_RQ, M_MBQ = 64, 80, 96, 112, 128, 144
M_SING = 160
M_PAIRSC = 176
M_RCOL = 184
M_INVB = 188
M_NW = 192
M_HEAD = 200
M_WRS = 264
M_FWS = 776
M_XSROW = 1800
M_GMAX = 1930
M_SDROW = 2048
M_FB = 2176


def build_nc(trunc=None, debug=False):
    nc = bass.Bass("TRN2", target_bir_lowering=False,
                   detect_race_conditions=False,
                   dynamic_dma_scratch_size=512)

    dx = nc.dram_tensor("x", [TOK, D], F32, kind="ExternalInput")
    dwq = nc.dram_tensor("wqt", [D, D], F32, kind="ExternalInput")
    dwk = nc.dram_tensor("wkt", [D, D], F32, kind="ExternalInput")
    dwv = nc.dram_tensor("wvt", [D, D], F32, kind="ExternalInput")
    dwo = nc.dram_tensor("wot", [D, D], F32, kind="ExternalInput")
    dwf = nc.dram_tensor("wft", [D, H], F32, kind="ExternalInput")
    ddel = nc.dram_tensor("delta", [H, 1], F32, kind="ExternalInput")
    dnw = nc.dram_tensor("nw", [128, NDB], F32, kind="ExternalInput")
    dfw = nc.dram_tensor("fw", [D, D], I8, kind="ExternalInput")
    dfws = nc.dram_tensor("fws", [1, D], F32, kind="ExternalInput")
    dfb = nc.dram_tensor("fb", [1, D], F32, kind="ExternalInput")
    dsm = nc.dram_tensor("segmask", [H, NCORE], F32, kind="ExternalInput")
    dout = nc.dram_tensor("out", [TOK, D], F32, kind="ExternalOutput")
    agi = nc.dram_tensor("agi", [128, 520], F32)
    ago = nc.dram_tensor("ago", [128 * NCORE, 520], F32)
    agi2 = nc.dram_tensor("agi2", [1, 8], F32)
    ago2 = nc.dram_tensor("ago2", [NCORE, 8], F32)
    if debug:
        dbg_tb = nc.dram_tensor("dbg_tb", [128, 512], F32, kind="ExternalOutput")
        dbg_misc = nc.dram_tensor("dbg_misc", [128, 3200], F32, kind="ExternalOutput")
        dbg_xres = nc.dram_tensor("dbg_xres", [128, NT * D], F32, kind="ExternalOutput")
        dbg_ktm = nc.dram_tensor("dbg_ktm", [128, NT * D], F32, kind="ExternalOutput")
        dbg_gb = nc.dram_tensor("dbg_gb", [128, 520], F32, kind="ExternalOutput")
        dbg_tb2 = nc.dram_tensor("dbg_tb2", [128, 512], F32, kind="ExternalOutput")

    bfnp = mybir.dt.np(BF16)
    c_i128b = nc.inline_tensor(np.eye(128, dtype=np.float32).astype(bfnp), name="c_i128b")
    c_i128f = nc.inline_tensor(np.eye(128, dtype=np.float32), name="c_i128f")
    c_i16f = nc.inline_tensor(np.eye(16, dtype=np.float32), name="c_i16f")
    mk = (np.arange(128)[None, :] >= np.arange(128)[:, None])
    c_mask = nc.inline_tensor(mk.astype(bfnp), name="c_mask")
    c_onescol = nc.inline_tensor(np.ones((128, 1), np.float32).astype(bfnp), name="c_onescol")
    c_onesrow = nc.inline_tensor(np.ones((1, 128), np.float32), name="c_onesrow")
    p2 = np.zeros((2, 128), np.float32)
    p2[0, 0:64] = 1.0
    p2[1, 64:128] = 1.0
    c_pat2 = nc.inline_tensor(p2, name="c_pat2")
    p16 = np.zeros((16, 128), np.float32)
    for h in range(16):
        p16[h, 64 * (h % 2): 64 * (h % 2) + 64] = 1.0
    c_pat16 = nc.inline_tensor(p16, name="c_pat16")
    pfm = np.zeros((16, 512), np.float32)
    for h in range(16):
        pfm[h, 64 * (h // 2): 64 * (h // 2) + 64] = 1.0
    c_patf = nc.inline_tensor(pfm, name="c_patf")
    pf8 = np.zeros((16, 8), np.float32)
    for h in range(16):
        pf8[h, h // 2] = 1.0
    c_patf8 = nc.inline_tensor(pf8, name="c_patf8")
    c_vals = nc.inline_tensor(np.tile(np.array([[1e-5, 1e-6, RND, 0.0]], np.float32), (128, 1)), name="c_vals")
    i16p = np.zeros((128, 16), np.float32)
    i16p[0:16, 0:16] = np.eye(16)
    c_i16pad = nc.inline_tensor(i16p, name="c_i16pad")
    orp = np.zeros((128, 128), np.float32)
    orp[0, :] = 1.0
    c_onesrowpad = nc.inline_tensor(orp, name="c_onesrowpad")
    e0 = np.zeros((128, 1), np.float32)
    e0[0, 0] = 1.0
    c_e0col = nc.inline_tensor(e0, name="c_e0col")
    p16p = np.zeros((128, 128), np.float32)
    p16p[0:16, :] = p16
    c_pat16pad = nc.inline_tensor(p16p, name="c_pat16pad")

    xres = nc.alloc_sbuf_tensor("xres", [128, NT * D], BF16)
    xnfm = nc.alloc_sbuf_tensor("xnfm", [128, NT * D], BF16)
    ktm = nc.alloc_sbuf_tensor("ktm", [128, NT * D], BF16)
    vtm = nc.alloc_sbuf_tensor("vtm", [128, NT * D], BF16)
    wA = nc.alloc_sbuf_tensor("wA", [128, NDB * D], BF16)
    wB = nc.alloc_sbuf_tensor("wB", [128, NDB * D], BF16)
    wstg = nc.alloc_sbuf_tensor("wstg", [128, D], F32)
    fstg = nc.alloc_sbuf_tensor("fstg", [128, D], I8)
    fpk = nc.alloc_sbuf_tensor("fpk", [128, 2048], F32)
    expl = nc.alloc_sbuf_tensor("expl", [128, 512], F32)
    ztm = nc.alloc_sbuf_tensor("ztm", [128, 256], F32)
    qsil = nc.alloc_sbuf_tensor("qsil", [128, D], BF16)
    qfA = nc.alloc_sbuf_tensor("qfA", [128, D], BF16)
    qfB = nc.alloc_sbuf_tensor("qfB", [128, D], BF16)
    kf = nc.alloc_sbuf_tensor("kf", [128, D], BF16)
    att = nc.alloc_sbuf_tensor("att", [128, 2048], BF16)
    osc = nc.alloc_sbuf_tensor("osc", [128, D], BF16)
    atn = nc.alloc_sbuf_tensor("atn", [128, D], F32)
    wfb = nc.alloc_sbuf_tensor("wfb", [128, 128], BF16)
    sprime = nc.alloc_sbuf_tensor("sprime", [128, 512], BF16)
    tbuf = nc.alloc_sbuf_tensor("tbuf", [128, 512], F32)
    ttmp = nc.alloc_sbuf_tensor("ttmp", [128, 512], F32)
    gb = nc.alloc_sbuf_tensor("gb", [128, 520], F32)
    sbig = nc.alloc_sbuf_tensor("sbig", [128, D], BF16)
    bbig = nc.alloc_sbuf_tensor("bbig", [128, D], BF16)
    misc = nc.alloc_sbuf_tensor("misc", [128, 3200], F32)
    si128b = nc.alloc_sbuf_tensor("si128b", [128, 128], BF16)
    si128f = nc.alloc_sbuf_tensor("si128f", [128, 128], F32)
    si16f = nc.alloc_sbuf_tensor("si16f", [16, 16], F32)
    smask = nc.alloc_sbuf_tensor("smask", [128, 128], BF16)
    sonec = nc.alloc_sbuf_tensor("sonec", [128, 1], BF16)
    spatf = nc.alloc_sbuf_tensor("spatf", [16, 512], F32)
    spatf8 = nc.alloc_sbuf_tensor("spatf8", [16, 8], F32)
    si16p = nc.alloc_sbuf_tensor("si16p", [128, 16], F32)
    sorp = nc.alloc_sbuf_tensor("sorp", [128, 128], F32)
    se0c = nc.alloc_sbuf_tensor("se0c", [128, 1], F32)
    spat16p = nc.alloc_sbuf_tensor("spat16p", [128, 128], F32)
    svals = nc.alloc_sbuf_tensor("svals", [128, 4], F32)
    nc.const_aps.register(F32, 1e-5, svals[:, 0:1])
    nc.const_aps.register(F32, 1e-6, svals[:, 1:2])
    nc.const_aps.register(F32, RND, svals[:, 2:3])

    ps_s = nc.alloc_psum_tensor("ps_s", [128, 512], F32)
    ps_a = nc.alloc_psum_tensor("ps_a", [128, 1024], F32)
    ps_b = nc.alloc_psum_tensor("ps_b", [128, 1024], F32)
    ps_t = nc.alloc_psum_tensor("ps_t", [128, 512], F32)
    ps_m = nc.alloc_psum_tensor("ps_m", [128, 512], F32)
    ps_t2 = nc.alloc_psum_tensor("ps_t2", [128, 512], F32)

    plan = []
    ckpt = {}
    sems = {}
    ctr = {}
    for name in ("ds", "da", "cc", "te", "ve", "ac", "gp"):
        sems[name] = nc.alloc_semaphore(name)
        ctr[name] = 0

    def OP(eng, meth, *a, inc=(), **k):
        incs = []
        for s, n in inc:
            ctr[s] += n
            incs.append((s, n))
        plan.append((eng, "op", meth, a, k, incs))

    def W(eng, sem, val=None):
        plan.append((eng, "w", sem, val if val is not None else ctr[sem], None, None))

    def VDR(inc=False):
        OP("vector", "drain", inc=((("ve", 1),) if inc else ()))
        return ctr["ve"]

    def ADR(inc=False):
        OP("scalar", "drain", inc=((("ac", 1),) if inc else ()))
        return ctr["ac"]

    def GDR(inc=False):
        OP("gpsimd", "drain", inc=((("gp", 1),) if inc else ()))
        return ctr["gp"]

    def mark(sem):
        return ctr[sem]

    OP("vector", "memset", fpk[:, :], 0.0)
    VDR()
    OP("vector", "memset", fpk[96:112, :], 1.0)
    m_fpk = VDR(True)
    OP("vector", "memset", misc[:, :], 0.0)
    OP("vector", "memset", qfA[64:128, :], 0.0)
    OP("vector", "memset", qfB[0:64, :], 0.0)
    OP("vector", "memset", gb[:, 512:520], 0.0)
    m_zero = VDR(True)
    W("sync", "ve", m_zero)
    for dst, src in ((si128b, c_i128b), (si128f, c_i128f), (si16f, c_i16f),
                     (smask, c_mask), (sonec, c_onescol),
                     (spatf, c_patf),
                     (spatf8, c_patf8), (svals, c_vals), (si16p, c_i16pad),
                     (sorp, c_onesrowpad), (se0c, c_e0col), (spat16p, c_pat16pad)):
        OP("sync", "dma_start", dst[:, :], src.ap(), inc=(("ds", 16),))
    OP("sync", "dma_start", misc[0:16, M_HEAD + 35:M_HEAD + 36], ddel.ap(), inc=(("ds", 16),), allow_slow_non_contiguous=True)
    OP("sync", "dma_start", misc[:, M_NW:M_NW + 8], dnw.ap(), inc=(("ds", 16),))
    OP("sync", "dma_start", misc[0:1, M_FWS:M_FWS + D], dfws.ap(), inc=(("ds", 16),))
    OP("sync", "dma_start", misc[0:1, M_FB:M_FB + D], dfb.ap(), inc=(("ds", 16),))
    OP("sync", "dma_start", misc[0:16, M_HEAD + 36:M_HEAD + 44], dsm.ap(), inc=(("ds", 16),))
    m_consts = mark("ds")
    ckpt['consts'] = len(plan)
    OP("vector", "memset", misc[0:1, M_GMAX:M_GMAX + 8], 0.0)
    m_ones16 = VDR(True)

    xcast = []
    W("scalar", "ve", m_fpk)
    for t in range(NT):
        half = t % 2
        if t >= 2:
            W("scalar", "gp", xcast[t - 2])
        OP("scalar", "dma_start", fpk[:, half * 1024:(half + 1) * 1024],
           dx[t * 128:(t + 1) * 128, :], inc=(("da", 16),))
        W("gpsimd", "da", mark("da"))
        OP("gpsimd", "tensor_copy", xres[:, t * D:(t + 1) * D],
           fpk[:, half * 1024:(half + 1) * 1024])
        xcast.append(GDR(True))
    ckpt['xload'] = len(plan)

    gp_cast = []

    def load_w(dram, slot, scale_col=None, pre_wait=None):
        marks = []
        for db in range(NDB):
            if gp_cast:
                W("sync", gp_cast[-1][0], gp_cast[-1][1])
            OP("sync", "dma_start", wstg[:, :], dram[db * 128:(db + 1) * 128, :],
               inc=(("ds", 16),))
            W("vector", "ds", mark("ds"))
            if pre_wait is not None and db == 0:
                W("vector", pre_wait[0], pre_wait[1])
            if scale_col is None:
                OP("vector", "tensor_copy", slot[:, db * D:(db + 1) * D], wstg[:, :])
            else:
                OP("vector", "tensor_scalar_mul", slot[:, db * D:(db + 1) * D],
                   wstg[:, :], scale_col(db))
            m = VDR(True)
            gp_cast.append(("ve", m))
            marks.append(m)
        return marks

    ckpt['prewk'] = len(plan)
    wk_marks = load_w(dwk, wA)
    ckpt['wk'] = len(plan)
    wv_marks = load_w(dwv, wB)
    W("sync", gp_cast[-1][0], gp_cast[-1][1])
    for db in range(NDB):
        OP("sync", "dma_start", wstg[:, db * 16:(db + 1) * 16],
           dwf[db * 128:(db + 1) * 128, :], inc=(("ds", 16),))
    W("gpsimd", "ds", mark("ds"))
    OP("gpsimd", "tensor_copy", wfb[:, 0:128], wstg[:, 0:128])
    m_wf = GDR(True)
    gp_cast.append(("gp", m_wf))

    ckpt['g0'] = len(plan)
    for t in range(NT):
        W("vector", "gp", xcast[t])
        OP("vector", "tensor_reduce", misc[:, M_ST1 + t:M_ST1 + t + 1],
           xres[:, t * D:(t + 1) * D], AX.X, AL.add)
        W("scalar", "gp", xcast[t])
        OP("scalar", "activation", qsil[:, :], xres[:, t * D:(t + 1) * D], AF.Square,
           0.0, 1.0, accum_out=misc[:, M_ST2 + t:M_ST2 + t + 1])
    m_ln1s_v = VDR(True)
    m_ln1s_a = ADR(True)

    def ln_finalize(eps=1e-5):
        OP("vector", "tensor_scalar_mul", misc[:, M_M3:M_M3 + 16], misc[:, M_ST1:M_ST1 + 16], 1.0 / D)
        OP("vector", "tensor_scalar_mul", misc[:, M_VAR3:M_VAR3 + 16], misc[:, M_ST2:M_ST2 + 16], 1.0 / D)
        VDR()
        OP("vector", "scalar_tensor_tensor", misc[:, M_ST2:M_ST2 + 16], misc[:, M_M3:M_M3 + 16],
           -1.0, misc[:, M_M3:M_M3 + 16], AL.mult, AL.mult)
        VDR()
        OP("vector", "tensor_add", misc[:, M_VAR3:M_VAR3 + 16], misc[:, M_VAR3:M_VAR3 + 16],
           misc[:, M_ST2:M_ST2 + 16])
        mv = VDR(True)
        W("scalar", "ve", mv)
        OP("scalar", "activation", misc[:, M_ST1:M_ST1 + 16], misc[:, M_VAR3:M_VAR3 + 16],
           AF.Sqrt, eps, 1.0)
        ma = ADR(True)
        W("vector", "ac", ma)
        OP("vector", "reciprocal", misc[:, M_R3:M_R3 + 16], misc[:, M_ST1:M_ST1 + 16])
        VDR()
        OP("vector", "scalar_tensor_tensor", misc[:, M_MB3:M_MB3 + 16], misc[:, M_M3:M_M3 + 16],
           -1.0, misc[:, M_R3:M_R3 + 16], AL.mult, AL.mult)
        return VDR(True)

    W("vector", "ac", m_ln1s_a)
    m_ln1f = ln_finalize()

    xn_marks = []
    for t in range(NT):
        W("scalar", "ve", m_ln1f)
        if t > 0:
            W("scalar", "ve", xn_marks[-1])
        OP("scalar", "activation", kf[:, :], xres[:, t * D:(t + 1) * D], AF.Identity,
           misc[:, M_MB3 + t:M_MB3 + t + 1], misc[:, M_R3 + t:M_R3 + t + 1],
           inc=(("ac", 1),))
        m_xn = mark("ac")
        for rnd in range(2):
            W("tensor", "ac", m_xn)
            if t > 0 or rnd > 0:
                W("tensor", "ve", ctr["ve"])
            if t == 0 and rnd == 0:
                W("tensor", "ds", m_consts)
            for j in range(4):
                db = rnd * 4 + j
                OP("tensor", "matmul", ps_t[:, j * 128:(j + 1) * 128],
                   kf[:, db * 128:(db + 1) * 128], si128b[:, :],
                   start=(j == 0), stop=(j == 3),
                   inc=((("te", 1),) if j == 3 else ()))
            W("vector", "te", mark("te"))
            out3 = bass.AP(xnfm, rnd * 4 * 2048 + t * 128,
                           [[NT * D, 128], [2048, 4], [1, 128]])
            in3 = ps_t[:, :].rearrange("p (b i) -> p b i", b=4)
            OP("vector", "tensor_copy", out3, in3, inc=(("ve", 1),))
        xn_marks.append(mark("ve"))

    ckpt['ln1'] = len(plan)
    W("tensor", "gp", m_wf)
    W("tensor", "ve", xn_marks[-1])
    for t in range(NT):
        for db in range(NDB):
            last = (t == NT - 1 and db == NDB - 1)
            OP("tensor", "matmul", ps_m[:, t * 16:(t + 1) * 16],
               xnfm[:, db * 2048 + t * 128: db * 2048 + (t + 1) * 128],
               wfb[:, db * 16:(db + 1) * 16],
               start=(t == 0 and db == 0), stop=last,
               inc=((("te", 1),) if last else ()))
    m_fmm = mark("te")
    W("vector", "te", m_fmm)
    OP("vector", "tensor_copy", ztm[:, 0:256], ps_m[:, 0:256], inc=(("ve", 1),))
    m_zev = mark("ve")

    for g in range(4):
        W("tensor", "ve", m_zev if g == 0 else ctr["ve"])
        for j in range(4):
            t = g * 4 + j
            OP("tensor", "matmul", ps_t[0:16, j * 128:(j + 1) * 128],
               ztm[:, t * 16:(t + 1) * 16], si128f[:, :],
               start=(j == 0), stop=(j == 3),
               inc=((("te", 1),) if j == 3 else ()))
        W("vector", "te", mark("te"))
        OP("vector", "tensor_copy", fpk[32:48, g * 512:(g + 1) * 512],
           ps_t[0:16, 0:512], inc=(("ve", 1),))
    m_zfm = mark("ve")
    W("scalar", "ve", m_zfm)
    W("scalar", "ds", m_consts)
    OP("scalar", "activation", fpk[64:80, :], fpk[32:48, :], AF.Sigmoid,
       misc[0:16, M_HEAD + 35:M_HEAD + 36], 1.0)
    ADR()
    OP("scalar", "activation", fpk[32:48, :], fpk[64:80, :], AF.Ln, 0.0, 1.0)
    m_lsig = ADR(True)
    W("vector", "ac", m_lsig)
    W("vector", "ve", m_ones16)
    OP("vector", "tensor_tensor_scan", fpk[0:16, :], fpk[32:48, :], fpk[32:48, :],
       0.0, AL.add, AL.bypass)
    VDR()
    OP("vector", "tensor_copy", misc[0:16, M_HEAD:M_HEAD + 1], fpk[0:16, 2047:2048])
    m_lfm = VDR(True)
    W("tensor", "ve", m_lfm)
    for c in range(NT):
        OP("tensor", "matmul", ps_m[:, c * 16:(c + 1) * 16],
           fpk[:, c * 128:(c + 1) * 128], si16p[:, :],
           start=(c == 0), stop=(c == NT - 1),
           inc=((("te", 1),) if c == NT - 1 else ()))
    m_ltr = mark("te")
    W("scalar", "te", m_ltr)
    OP("scalar", "activation", expl[:, 0:256], ps_m[:, 0:256], AF.Exp, 0.0, 1.0)
    ADR()
    OP("scalar", "activation", expl[:, 256:512], ps_m[:, 0:256], AF.Exp, 0.0, -1.0)
    m_expl = ADR(True)

    ckpt['fL'] = len(plan)
    kv_ev_a = None
    kv_ev_b = None
    ktm_marks = []
    for c in range(NT):
        if kv_ev_a is not None:
            W("tensor", "ac", kv_ev_a)
            W("tensor", "ve", ktm_marks[-1])
        else:
            W("tensor", "ve", wk_marks[-1])
            W("tensor", "ve", wv_marks[-1])
        for db in range(NDB):
            lhs = xnfm[:, db * 2048 + c * 128: db * 2048 + (c + 1) * 128]
            for jh in range(2):
                last = (db == NDB - 1 and jh == 1)
                OP("tensor", "matmul", ps_a[:, jh * 512:(jh + 1) * 512], lhs,
                   wA[:, db * D + jh * 512: db * D + (jh + 1) * 512],
                   start=(db == 0), stop=(db == NDB - 1),
                   inc=((("te", 1),) if last else ()))
        m_kmm = mark("te")
        if kv_ev_b is not None:
            W("tensor", "ac", kv_ev_b)
        for db in range(NDB):
            lhs = xnfm[:, db * 2048 + c * 128: db * 2048 + (c + 1) * 128]
            for jh in range(2):
                last = (db == NDB - 1 and jh == 1)
                OP("tensor", "matmul", ps_b[:, jh * 512:(jh + 1) * 512], lhs,
                   wB[:, db * D + jh * 512: db * D + (jh + 1) * 512],
                   start=(db == 0), stop=(db == NDB - 1),
                   inc=((("te", 1),) if last else ()))
        m_vmm = mark("te")
        W("scalar", "te", m_kmm)
        if c > 0:
            W("scalar", "ve", ktm_marks[-1])
        if c == 0:
            W("scalar", "ac", m_expl)
        OP("scalar", "activation", qsil[:, :], ps_a[:, :], AF.Silu, 0.0, 1.0,
           inc=(("ac", 1),))
        kv_ev_a = mark("ac")
        W("vector", "ac", kv_ev_a)
        o4 = ktm[:, c * D:(c + 1) * D].rearrange("p (b h d) -> p b h d", b=NDB, h=2)
        i4 = qsil[:, :].rearrange("p (b h d) -> p b h d", b=NDB, h=2)
        e4 = expl[:, 256 + c * 16:256 + (c + 1) * 16].rearrange(
            "p (b h d) -> p b h d", b=NDB, h=2, d=1)
        e4b, i4b = bass.broadcast_tensor_aps(e4, i4)
        OP("vector", "tensor_tensor", o4, i4b, e4b, AL.mult, inc=(("ve", 1),))
        ktm_marks.append(mark("ve"))
        W("scalar", "te", m_vmm)
        OP("scalar", "copy", vtm[:, c * D:(c + 1) * D], ps_b[:, :],
           inc=(("ac", 1),))
        kv_ev_b = mark("ac")
        W("tensor", "ve", ktm_marks[-1])
        W("tensor", "ac", kv_ev_b)
        for p in range(NDB):
            for b in range(2):
                h = 2 * p + b
                last = (p == NDB - 1 and b == 1)
                OP("tensor", "matmul", ps_s[b * 64:(b + 1) * 64, p * 64:(p + 1) * 64],
                   ktm[:, c * D + h * 64: c * D + (h + 1) * 64],
                   vtm[:, c * D + h * 64: c * D + (h + 1) * 64],
                   start=(c == 0 and p == 0), stop=(c == NT - 1 and p == NDB - 1),
                   tile_position=(0, b * 64), skip_group_check=True,
                   inc=((("te", 1),) if last else ()))
        m_u = mark("te")
    m_sweep1 = m_u

    ckpt['sweep1'] = len(plan)
    wq_marks = load_w(dwq, wA, pre_wait=("te", m_sweep1))
    W("scalar", "ve", m_lfm)
    OP("scalar", "activation", misc[0:16, M_HEAD + 1:M_HEAD + 2],
       misc[0:16, M_HEAD:M_HEAD + 1], AF.Exp, 0.0, 1.0)
    m_lamE = ADR(True)
    W("vector", "ac", m_lamE)
    W("vector", "ds", m_consts)
    OP("vector", "tensor_scalar_mul", misc[0:16, M_HEAD + 45:M_HEAD + 53],
       spatf8[:, :], misc[0:16, M_HEAD + 1:M_HEAD + 2])
    m_pfe = VDR(True)
    W("tensor", "ve", m_pfe)
    OP("tensor", "matmul", ps_m[:, 0:8], spat16p[:, :],
       misc[:, M_HEAD + 45:M_HEAD + 53], start=True, stop=True,
       inc=(("te", 1),))
    W("vector", "te", mark("te"))
    OP("vector", "tensor_copy", misc[:, M_PAIRSC:M_PAIRSC + 8], ps_m[:, 0:8])
    m_pairsc = VDR(True)
    W("scalar", "ve", m_pairsc)
    for p in range(NDB):
        OP("scalar", "activation", gb[:, p * 64:(p + 1) * 64],
           ps_s[:, p * 64:(p + 1) * 64], AF.Identity, 0.0,
           misc[:, M_PAIRSC + p:M_PAIRSC + p + 1])
    m_bsend = ADR(True)
    W("vector", "ac", m_bsend)
    W("vector", "ve", m_ones16)
    OP("vector", "tensor_copy", gb[0:16, 512:513], misc[0:16, M_HEAD:M_HEAD + 1],
       inc=(("ve", 1),))
    m_gblam = mark("ve")
    W("sync", "ve", m_gblam)
    OP("sync", "dma_start", agi[0:128, 0:520], gb[:, 0:520], inc=(("ds", 16),))
    m_agdma = mark("ds")
    W("gpsimd", "ds", m_agdma)
    OP("gpsimd", "collective_compute", "AllGather", AL.bypass,
       replica_groups=[list(range(NCORE))], ins=[agi.ap().opt()],
       outs=[ago.ap().opt()], inc=(("cc", 1),))
    m_cc1 = mark("cc")
    ckpt['exch'] = len(plan)
    W("sync", "cc", m_cc1)
    for s in range(NCORE):
        OP("sync", "dma_start", misc[0:16, M_HEAD + 2 + s:M_HEAD + 3 + s],
           ago[s * 128:s * 128 + 16, 512:513], inc=(("ds", 16),),
           allow_slow_non_contiguous=True)
    m_lamall = mark("ds")
    q_marks = []
    ev_a = kv_ev_a
    for t in range(NT):
        if t == 0:
            W("tensor", "ve", wq_marks[-1])
        W("tensor", "ac", ev_a)
        for db in range(NDB):
            lhs = xnfm[:, db * 2048 + t * 128: db * 2048 + (t + 1) * 128]
            for jh in range(2):
                last = (db == NDB - 1 and jh == 1)
                OP("tensor", "matmul", ps_a[:, jh * 512:(jh + 1) * 512], lhs,
                   wA[:, db * D + jh * 512: db * D + (jh + 1) * 512],
                   start=(db == 0), stop=(db == NDB - 1),
                   inc=((("te", 1),) if last else ()))
        m_qmm = mark("te")
        W("scalar", "te", m_qmm)
        if q_marks:
            W("scalar", "ve", q_marks[-1])
        OP("scalar", "activation", qsil[:, :], ps_a[:, :], AF.Silu, 0.0, 1.0,
           inc=(("ac", 1),))
        ev_a = mark("ac")
        W("vector", "ac", ev_a)
        out4 = bass.AP(xnfm, t * 128, [[NT * D, 128], [2048, NDB], [64, 2], [1, 64]])
        i4 = qsil[:, :].rearrange("p (b h d) -> p b h d", b=NDB, h=2)
        e4 = expl[:, t * 16:(t + 1) * 16].rearrange("p (b h d) -> p b h d",
                                                    b=NDB, h=2, d=1)
        e4b, i4b = bass.broadcast_tensor_aps(e4, i4)
        OP("vector", "tensor_tensor", out4, i4b, e4b, AL.mult, inc=(("ve", 1),))
        q_marks.append(mark("ve"))
    m_qall = mark("ve")

    ckpt['bulkq'] = len(plan)
    W("vector", "ds", m_lamall)
    W("vector", "ve", m_ones16)
    OP("vector", "tensor_tensor_scan", misc[0:16, M_HEAD + 10:M_HEAD + 18],
       misc[0:16, M_HEAD + 2:M_HEAD + 10], misc[0:16, M_HEAD + 2:M_HEAD + 10],
       0.0, AL.add, AL.bypass)
    VDR()
    OP("vector", "tensor_mul", misc[0:16, M_HEAD + 19:M_HEAD + 27],
       misc[0:16, M_HEAD + 2:M_HEAD + 10], misc[0:16, M_HEAD + 36:M_HEAD + 44])
    VDR()
    OP("vector", "tensor_reduce", misc[0:16, M_HEAD + 18:M_HEAD + 19],
       misc[0:16, M_HEAD + 19:M_HEAD + 27], AX.X, AL.add)
    VDR()
    OP("vector", "tensor_scalar", misc[0:16, M_HEAD + 45:M_HEAD + 53],
       misc[0:16, M_HEAD + 10:M_HEAD + 18], -1.0,
       misc[0:16, M_HEAD + 18:M_HEAD + 19], AL.mult, AL.add)
    VDR()
    OP("vector", "tensor_scalar", misc[0:16, M_HEAD + 45:M_HEAD + 53],
       misc[0:16, M_HEAD + 45:M_HEAD + 53], 0.0, None, AL.min)
    m_wm1 = VDR(True)
    W("scalar", "ve", m_wm1)
    OP("scalar", "activation", misc[0:16, M_HEAD + 19:M_HEAD + 27],
       misc[0:16, M_HEAD + 45:M_HEAD + 53], AF.Exp, 0.0, 1.0)
    m_wraw = ADR(True)
    W("vector", "ac", m_wraw)
    OP("vector", "tensor_mul", misc[0:16, M_HEAD + 27:M_HEAD + 35],
       misc[0:16, M_HEAD + 19:M_HEAD + 27], misc[0:16, M_HEAD + 36:M_HEAD + 44])
    m_wmat = VDR(True)

    t_marks = []
    for s in range(NCORE):
        if t_marks:
            W("sync", "ve", t_marks[-1])
        OP("sync", "dma_start", gb[:, 0:512], ago[s * 128:(s + 1) * 128, 0:512],
           inc=(("ds", 16),))
        m_gbl = mark("ds")
        if s == 0:
            W("vector", "ve", m_wmat)
        OP("vector", "tensor_scalar_mul", misc[0:16, M_WRS:M_WRS + 512],
           spatf[:, :], misc[0:16, M_HEAD + 27 + s:M_HEAD + 28 + s])
        m_wrs = VDR(True)
        W("tensor", "ve", m_wrs)
        OP("tensor", "matmul", ps_m[:, 0:512], spat16p[:, :],
           misc[:, M_WRS:M_WRS + 512], start=True, stop=True,
           inc=(("te", 1),))
        m_wtile = mark("te")
        W("vector", "te", m_wtile)
        W("vector", "ds", m_gbl)
        if s == 0:
            OP("vector", "tensor_mul", tbuf[:, :], gb[:, 0:512], ps_m[:, 0:512])
        else:
            OP("vector", "tensor_mul", ttmp[:, :], gb[:, 0:512], ps_m[:, 0:512])
            VDR()
            OP("vector", "tensor_add", tbuf[:, :], tbuf[:, :], ttmp[:, :])
        t_marks.append(VDR(True))
    m_T = t_marks[-1]
    if debug:
        W("scalar", "ve", m_T)
        OP("scalar", "dma_start", dbg_tb[:, :], tbuf[:, :], inc=(("da", 16),))
    W("vector", "ve", m_T)
    OP("vector", "tensor_copy", sprime[:, :], tbuf[:, :], inc=(("ve", 1),))
    m_sp0 = mark("ve")

    wo_marks = load_w(dwo, wB, scale_col=lambda db: misc[:, M_NW + db:M_NW + db + 1],
                      pre_wait=("te", m_sweep1))

    ckpt['T'] = len(plan)
    m_attnev = None
    m_psbfree = None
    m_oev = None
    tr_bank_ev = [None, None]
    sp_mark = m_sp0
    m_rcol_prev = None
    res_marks = []
    for c in range(NT):
        ckpt[f's2c{c}'] = len(plan)
        trk = 0
        for which in ("q", "k"):
            for rnd in range(2):
                pst_ = ps_t if trk % 2 == 0 else ps_t2
                W("tensor", "ve", q_marks[c] if which == "q" else ktm_marks[c])
                bank_id = trk % 2
                if tr_bank_ev[bank_id] is not None:
                    W("tensor", "ac", tr_bank_ev[bank_id])
                for j in range(4):
                    db = rnd * 4 + j
                    if which == "q":
                        lhs = bass.AP(xnfm, db * 2048 + c * 128,
                                      [[NT * D, 128], [1, 128]])
                    else:
                        lhs = ktm[:, c * D + db * 128: c * D + (db + 1) * 128]
                    OP("tensor", "matmul", pst_[:, j * 128:(j + 1) * 128], lhs,
                       si128b[:, :], start=(j == 0), stop=(j == 3),
                       inc=((("te", 1),) if j == 3 else ()))
                W("scalar", "te", mark("te"))
                if which == "q":
                    OP("scalar", "copy", qfA[0:64, rnd * 512:(rnd + 1) * 512],
                       pst_[0:64, :])
                    OP("scalar", "copy", qfB[64:128, rnd * 512:(rnd + 1) * 512],
                       pst_[64:128, :], inc=(("ac", 1),))
                else:
                    OP("scalar", "copy", kf[:, rnd * 512:(rnd + 1) * 512],
                       pst_[:, :], inc=(("ac", 1),))
                tr_bank_ev[bank_id] = mark("ac")
                trk += 1
        m_tr = mark("ac")
        ckpt[f's2tr{c}'] = len(plan)
        att_marks = []
        for g in range(4):
            W("tensor", "ac", m_tr)
            if g >= 2:
                W("tensor", "ve", att_marks[g - 2])
            elif m_psbfree is not None:
                W("tensor", "ve", m_psbfree)
            half = g % 2
            for j in range(4):
                h = g * 4 + j
                p, b = h // 2, h % 2
                qsrc = qfA if b == 0 else qfB
                OP("tensor", "matmul",
                   ps_b[:, half * 512 + j * 128: half * 512 + (j + 1) * 128],
                   kf[:, p * 128:(p + 1) * 128],
                   qsrc[:, p * 128:(p + 1) * 128],
                   start=True, stop=True,
                   inc=((("te", 1),) if j == 3 else ()))
            W("vector", "te", mark("te"))
            i3 = ps_b[:, half * 512:(half + 1) * 512].rearrange("p (g i) -> p g i", g=4)
            m3 = smask[:, :].rearrange("p (g i) -> p g i", g=1)
            m3b, i3b = bass.broadcast_tensor_aps(m3, i3)
            o3 = att[:, g * 512:(g + 1) * 512].rearrange("p (g i) -> p g i", g=4)
            OP("vector", "tensor_tensor", o3, i3b, m3b, AL.mult, inc=(("ve", 1),))
            att_marks.append(mark("ve"))
        ckpt[f's2at{c}'] = len(plan)
        if m_oev is not None:
            W("tensor", "ac", m_oev)
        W("tensor", "ve", att_marks[-1])
        W("tensor", "ve", sp_mark)
        if c == 0:
            W("tensor", "ac", m_bsend)
        for p in range(NDB):
            for b in range(2):
                h = 2 * p + b
                OP("tensor", "matmul",
                   ps_a[b * 64:(b + 1) * 64, p * 128:(p + 1) * 128],
                   vtm[:, c * D + h * 64: c * D + (h + 1) * 64],
                   att[:, h * 128:(h + 1) * 128],
                   start=(p % 4 == 0), stop=False, tile_position=(0, b * 64),
                   skip_group_check=True)
            for b in range(2):
                qsrc = qfA if b == 0 else qfB
                last = (p == NDB - 1 and b == 1)
                OP("tensor", "matmul",
                   ps_a[b * 64:(b + 1) * 64, p * 128:(p + 1) * 128],
                   sprime[:, p * 64:(p + 1) * 64],
                   qsrc[:, p * 128:(p + 1) * 128],
                   start=False, stop=(p % 4 == 3), tile_position=(0, b * 64),
                   skip_group_check=True,
                   inc=((("te", 1),) if last else ()))
        m_omm = mark("te")
        if c > 0:
            W("tensor", "ve", sp_mark)
        for p in range(NDB):
            for b in range(2):
                h = 2 * p + b
                last = (p == NDB - 1 and b == 1)
                OP("tensor", "matmul", ps_s[b * 64:(b + 1) * 64, p * 64:(p + 1) * 64],
                   ktm[:, c * D + h * 64: c * D + (h + 1) * 64],
                   vtm[:, c * D + h * 64: c * D + (h + 1) * 64],
                   start=(c == 0 and p == 0), stop=(c == NT - 1 and p == NDB - 1),
                   tile_position=(0, b * 64), skip_group_check=True,
                   inc=((("te", 1),) if last else ()))
        m_umm = mark("te")
        W("vector", "te", m_umm)
        OP("vector", "tensor_add", sprime[:, :], ps_s[:, :], tbuf[:, :],
           inc=(("ve", 1),))
        sp_mark = mark("ve")
        ckpt[f's2u{c}'] = len(plan)
        W("scalar", "te", m_omm)
        OP("scalar", "copy", osc[:, :], ps_a[:, :], inc=(("ac", 1),))
        m_oev = mark("ac")
        OP("scalar", "activation", qsil[:, :], osc[:, :], AF.Square, 0.0, 1.0,
           inc=(("ac", 1),))
        m_osq = mark("ac")
        ckpt[f's2r{c}'] = len(plan)
        if c == 0:
            W("tensor", "ve", wo_marks[-1])
        W("tensor", "ve", att_marks[-1])
        if m_psbfree is not None:
            W("tensor", "ve", m_psbfree)
        W("tensor", "ac", m_oev)
        for db in range(NDB):
            for jh in range(2):
                last = (db == NDB - 1 and jh == 1)
                OP("tensor", "matmul", ps_b[:, jh * 512:(jh + 1) * 512],
                   osc[:, db * 128:(db + 1) * 128],
                   wB[:, db * D + jh * 512: db * D + (jh + 1) * 512],
                   start=(db == 0), stop=(db == NDB - 1),
                   inc=((("te", 1),) if last else ()))
        m_womm = mark("te")
        W("tensor", "ac", m_osq)
        if m_rcol_prev is not None:
            W("tensor", "ac", m_rcol_prev[0])
        for db in range(NDB):
            OP("tensor", "matmul", ps_m[:, 0:1],
               qsil[:, db * 128:(db + 1) * 128], sonec[:, :],
               start=(db == 0), stop=(db == NDB - 1),
               inc=((("te", 1),) if db == NDB - 1 else ()))
        m_ssum = mark("te")
        W("scalar", "te", m_ssum)
        OP("scalar", "activation", misc[:, M_RCOL + 1:M_RCOL + 2],
           ps_m[:, 0:1], AF.Sqrt, 1e-6, 1.0 / D, inc=(("ac", 1),))
        m_sd = mark("ac")
        m_rcol_prev = (m_sd, None)
        W("vector", "ac", m_sd)
        OP("vector", "reciprocal", misc[:, M_RCOL:M_RCOL + 1],
           misc[:, M_RCOL + 1:M_RCOL + 2])
        m_rcol = VDR(True)
        W("vector", "te", m_womm)
        if res_marks:
            W("vector", "ve", res_marks[-1])
        OP("vector", "tensor_scalar", atn[:, :], ps_b[:, :], 1.0, 0.0,
           AL.mult, AL.add, accum_out=misc[:, M_SING:M_SING + 1],
           inc=(("ve", 1),))
        m_psbfree = mark("ve")
        VDR()
        W("scalar", "te", m_ssum)
        W("scalar", "ve", m_psbfree)
        OP("scalar", "activation", qsil[:, :], atn[:, :], AF.Square, 0.0, 1.0,
           accum_out=misc[:, M_SING + 1:M_SING + 2], inc=(("ac", 1),))
        W("vector", "ac", mark("ac"))
        OP("vector", "tensor_scalar_mul", misc[:, M_SING + 2:M_SING + 3],
           misc[:, M_SING:M_SING + 1], 1.0 / D)
        OP("vector", "tensor_scalar_mul", misc[:, M_SING + 3:M_SING + 4],
           misc[:, M_SING + 1:M_SING + 2], 1.0 / D)
        VDR()
        OP("vector", "scalar_tensor_tensor", misc[:, M_SING + 4:M_SING + 5],
           misc[:, M_SING + 2:M_SING + 3], -1.0, misc[:, M_SING + 2:M_SING + 3],
           AL.mult, AL.mult)
        VDR()
        OP("vector", "tensor_add", misc[:, M_SING + 3:M_SING + 4],
           misc[:, M_SING + 3:M_SING + 4], misc[:, M_SING + 4:M_SING + 5])
        VDR()
        OP("vector", "scalar_tensor_tensor", misc[:, M_SING + 4:M_SING + 5],
           misc[:, M_RCOL:M_RCOL + 1], 1.0, misc[:, M_RCOL:M_RCOL + 1],
           AL.mult, AL.mult)
        VDR()
        OP("vector", "tensor_mul", misc[:, M_SING + 3:M_SING + 4],
           misc[:, M_SING + 3:M_SING + 4], misc[:, M_SING + 4:M_SING + 5])
        mv = VDR(True)
        W("scalar", "ve", mv)
        OP("scalar", "activation", misc[:, M_SING + 5:M_SING + 6],
           misc[:, M_SING + 3:M_SING + 4], AF.Sqrt, 1e-5, 1.0, inc=(("ac", 1),))
        W("vector", "ac", mark("ac"))
        OP("vector", "reciprocal", misc[:, M_SING + 6:M_SING + 7],
           misc[:, M_SING + 5:M_SING + 6])
        VDR()
        OP("vector", "tensor_mul", misc[:, M_SING + 6:M_SING + 7],
           misc[:, M_SING + 6:M_SING + 7], misc[:, M_RCOL:M_RCOL + 1])
        VDR()
        OP("vector", "scalar_tensor_tensor", misc[:, M_SING + 7:M_SING + 8],
           misc[:, M_SING + 2:M_SING + 3], -1.0, misc[:, M_SING + 6:M_SING + 7],
           AL.mult, AL.mult)
        VDR()
        mvv = VDR(True)
        W("scalar", "ve", mvv)
        OP("scalar", "activation", fpk[:, 0:1024], atn[:, :], AF.Identity,
           misc[:, M_SING + 7:M_SING + 8], misc[:, M_SING + 6:M_SING + 7],
           inc=(("ac", 1),))
        W("vector", "ac", mark("ac"))
        OP("vector", "engine_nop", inc=(("ve", 1),))
        m_attnev = mark("ve")
        ckpt[f's2w{c}'] = len(plan)
        OP("vector", "tensor_add", xres[:, c * D:(c + 1) * D],
           xres[:, c * D:(c + 1) * D], fpk[:, 0:1024])
        res_marks.append(VDR(True))

    ckpt['sweep2'] = len(plan)
    fw_marks = []
    for db in range(NDB):
        if fw_marks:
            W("sync", "gp", fw_marks[-1])
        OP("sync", "dma_start", fstg[:, :], dfw[db * 128:(db + 1) * 128, :],
           inc=(("ds", 16),))
        W("gpsimd", "ds", mark("ds"))
        if db == 0:
            W("gpsimd", "ve", m_qall)
        OP("gpsimd", "tensor_copy", wA[:, db * D:(db + 1) * D], fstg[:, :])
        fw_marks.append(GDR(True))

    for t in range(NT):
        OP("vector", "tensor_reduce", misc[:, M_ST1 + t:M_ST1 + t + 1],
           xres[:, t * D:(t + 1) * D], AX.X, AL.add)
        OP("vector", "tensor_reduce", misc[:, M_MX + t:M_MX + t + 1],
           xres[:, t * D:(t + 1) * D], AX.X, AL.max)
        OP("vector", "tensor_reduce", misc[:, M_MN + t:M_MN + t + 1],
           xres[:, t * D:(t + 1) * D], AX.X, AL.min)
        W("scalar", "ve", res_marks[t])
        W("scalar", "ve", m_attnev)
        OP("scalar", "activation", qsil[:, :], xres[:, t * D:(t + 1) * D], AF.Square,
           0.0, 1.0, accum_out=misc[:, M_ST2 + t:M_ST2 + t + 1])
    VDR(True)
    m2a = ADR(True)
    W("vector", "ac", m2a)
    m_ln2f = ln_finalize()
    W("vector", "ve", m_ln2f)
    OP("vector", "tensor_sub", misc[:, M_ST1:M_ST1 + 16], misc[:, M_MX:M_MX + 16],
       misc[:, M_M3:M_M3 + 16])
    OP("vector", "tensor_sub", misc[:, M_ST2:M_ST2 + 16], misc[:, M_M3:M_M3 + 16],
       misc[:, M_MN:M_MN + 16])
    VDR()
    OP("vector", "tensor_max", misc[:, M_ST1:M_ST1 + 16], misc[:, M_ST1:M_ST1 + 16],
       misc[:, M_ST2:M_ST2 + 16])
    VDR()
    OP("vector", "tensor_mul", misc[:, M_ST1:M_ST1 + 16], misc[:, M_ST1:M_ST1 + 16],
       misc[:, M_R3:M_R3 + 16])
    VDR()
    OP("vector", "tensor_reduce", misc[:, M_ST2:M_ST2 + 1], misc[:, M_ST1:M_ST1 + 16],
       AX.X, AL.max)
    m_rowabs = VDR(True)
    W("tensor", "ve", m_rowabs)
    OP("tensor", "matmul", ps_m[0:1, 256:384], misc[:, M_ST2:M_ST2 + 1],
       si128f[:, :], start=True, stop=True, inc=(("te", 1),))
    W("vector", "te", mark("te"))
    OP("vector", "tensor_reduce", misc[0:1, M_GMAX:M_GMAX + 1],
       ps_m[0:1, 256:384], AX.X, AL.max)
    m_locmax = VDR(True)
    W("sync", "ve", m_locmax)
    W("sync", "ve", m_ones16)
    OP("sync", "dma_start", agi2[0:1, 0:8], misc[0:1, M_GMAX:M_GMAX + 8],
       inc=(("ds", 16),))
    W("gpsimd", "ds", mark("ds"))
    OP("gpsimd", "collective_compute", "AllGather", AL.bypass,
       replica_groups=[list(range(NCORE))], ins=[agi2.ap().opt()],
       outs=[ago2.ap().opt()], inc=(("cc", 1),))
    m_cc2 = mark("cc")
    W("sync", "cc", m_cc2)
    OP("sync", "dma_start", misc[0:8, M_HEAD + 44:M_HEAD + 45], ago2[0:8, 0:1],
       inc=(("ds", 16),), allow_slow_non_contiguous=True)
    m_gm8 = mark("ds")
    W("tensor", "ds", m_gm8)
    OP("tensor", "matmul", ps_m[0:1, 384:392], misc[0:8, M_HEAD + 44:M_HEAD + 45],
       si16f[0:8, 0:8], start=True, stop=True, inc=(("te", 1),))
    W("vector", "te", mark("te"))
    OP("vector", "tensor_reduce", misc[0:1, M_GMAX + 1:M_GMAX + 2],
       ps_m[0:1, 384:392], AX.X, AL.max)
    VDR()
    OP("vector", "tensor_scalar", misc[0:1, M_GMAX + 2:M_GMAX + 3],
       misc[0:1, M_GMAX + 1:M_GMAX + 2], 1.0 / 127.0, 1e-5, AL.mult, AL.max)
    VDR()
    OP("vector", "reciprocal", misc[0:1, M_GMAX + 3:M_GMAX + 4],
       misc[0:1, M_GMAX + 2:M_GMAX + 3])
    m_xs = VDR(True)
    W("tensor", "ve", m_xs)
    OP("tensor", "matmul", ps_m[:, 392:393], sorp[:, :],
       misc[:, M_GMAX + 3:M_GMAX + 4], start=True, stop=True)
    OP("tensor", "matmul", ps_m[0:1, 136:264],
       misc[:, M_GMAX + 2:M_GMAX + 3], sorp[:, :], start=True, stop=True,
       inc=(("te", 1),))
    W("vector", "te", mark("te"))
    OP("vector", "tensor_copy", misc[:, M_INVB:M_INVB + 1], ps_m[:, 392:393])
    VDR()
    OP("vector", "tensor_copy", misc[0:1, M_XSROW:M_XSROW + 128], ps_m[0:1, 136:264])
    VDR()
    OP("vector", "tensor_scalar_mul", misc[:, M_RQ:M_RQ + 16],
       misc[:, M_R3:M_R3 + 16], misc[:, M_INVB:M_INVB + 1])
    OP("vector", "tensor_scalar_mul", misc[:, M_MBQ:M_MBQ + 16],
       misc[:, M_MB3:M_MB3 + 16], misc[:, M_INVB:M_INVB + 1])
    m_quantc = VDR(True)
    m_last_ev = m_quantc
    W("tensor", "ve", m_quantc)
    for jh in range(2):
        OP("tensor", "matmul", ps_t[:, 0:512],
           misc[:, M_XSROW:M_XSROW + 128],
           misc[:, M_FWS + jh * 512:M_FWS + (jh + 1) * 512],
           start=True, stop=True, inc=(("te", 1),))
        W("vector", "te", mark("te"))
        OP("vector", "tensor_copy", sbig[:, jh * 512:(jh + 1) * 512], ps_t[:, 0:512],
           inc=(("ve", 1),))
        W("tensor", "ve", mark("ve"))
    for jh in range(2):
        OP("tensor", "matmul", ps_t[:, 0:512], sorp[:, :],
           misc[:, M_FB + jh * 512:M_FB + (jh + 1) * 512],
           start=True, stop=True, inc=(("te", 1),))
        W("vector", "te", mark("te"))
        OP("vector", "tensor_copy", bbig[:, jh * 512:(jh + 1) * 512], ps_t[:, 0:512],
           inc=(("ve", 1),))
        if jh == 0:
            W("tensor", "ve", mark("ve"))
    m_sbig = mark("ve")

    m_ffnev = {}
    m_psfree = {}
    prev_out_dma = None
    prev_tr = None
    def emit_quant(t, tr_done_mark):
        OP("scalar", "activation", osc[:, :], xres[:, t * D:(t + 1) * D], AF.Identity,
           misc[:, M_MBQ + t:M_MBQ + t + 1], misc[:, M_RQ + t:M_RQ + t + 1],
           inc=(("ac", 1),))
        W("vector", "ac", mark("ac"))
        if tr_done_mark is not None:
            W("vector", "te", tr_done_mark)
        OP("vector", "tensor_scalar", kf[:, :], osc[:, :], RND, RND, AL.add,
           AL.subtract, inc=(("ve", 1),))
        return mark("ve")

    W("scalar", "ve", m_quantc)
    rnd_marks = {0: emit_quant(0, None)}
    m_trlast = None
    for t in range(NT):
        pst = ps_a if t % 2 == 0 else ps_b
        m_rnd = rnd_marks[t]
        for rnd in range(2):
            pst_ = ps_t if rnd == 0 else ps_t2
            W("tensor", "ve", m_rnd)
            if tr_bank_ev[rnd] is not None:
                W("tensor", "ac", tr_bank_ev[rnd])
            if t == 0 and rnd == 0:
                W("tensor", "ve", m_sbig)
            for j in range(4):
                db = rnd * 4 + j
                OP("tensor", "matmul", pst_[:, j * 128:(j + 1) * 128],
                   kf[:, db * 128:(db + 1) * 128], si128b[:, :],
                   start=(j == 0), stop=(j == 3),
                   inc=((("te", 1),) if j == 3 else ()))
            m_trlast = mark("te")
            W("scalar", "te", m_trlast)
            OP("scalar", "copy", qfA[:, rnd * 512:(rnd + 1) * 512], pst_[:, :],
               inc=(("ac", 1),))
            tr_bank_ev[rnd] = mark("ac")
        prev_tr = mark("ac")
        W("tensor", "ac", prev_tr)
        if t == 0:
            W("tensor", "gp", fw_marks[-1])
        if t >= 2:
            W("tensor", "ve", m_psfree[t - 2])
        for db in range(NDB):
            for jh in range(2):
                last = (db == NDB - 1 and jh == 1)
                OP("tensor", "matmul", pst[:, jh * 512:(jh + 1) * 512],
                   qfA[:, db * 128:(db + 1) * 128],
                   wA[:, db * D + jh * 512: db * D + (jh + 1) * 512],
                   start=(db == 0), stop=(db == NDB - 1),
                   inc=((("te", 1),) if last else ()))
        m_fmm2 = mark("te")
        if t + 1 < NT:
            rnd_marks[t + 1] = emit_quant(t + 1, m_trlast)
        W("vector", "te", m_fmm2)
        OP("vector", "tensor_mul", atn[:, :], pst[:, :], sbig[:, :],
           inc=(("ve", 1),))
        m_psfree[t] = mark("ve")
        VDR()
        if prev_out_dma is not None:
            W("vector", "da", prev_out_dma)
        OP("vector", "scalar_tensor_tensor", fpk[:, 0:1024], atn[:, :], 1.0,
           bbig[:, :], AL.mult, AL.add, accum_out=misc[:, M_SING:M_SING + 1])
        VDR(True)
        W("scalar", "ve", mark("ve"))
        OP("scalar", "activation", qsil[:, :], fpk[:, 0:1024], AF.Square, 0.0, 1.0,
           accum_out=misc[:, M_SING + 1:M_SING + 2], inc=(("ac", 1),))
        W("vector", "ac", mark("ac"))
        OP("vector", "tensor_scalar_mul", misc[:, M_SING + 2:M_SING + 3],
           misc[:, M_SING:M_SING + 1], 1.0 / D)
        OP("vector", "tensor_scalar_mul", misc[:, M_SING + 3:M_SING + 4],
           misc[:, M_SING + 1:M_SING + 2], 1.0 / D)
        VDR()
        OP("vector", "scalar_tensor_tensor", misc[:, M_SING + 4:M_SING + 5],
           misc[:, M_SING + 2:M_SING + 3], -1.0, misc[:, M_SING + 2:M_SING + 3],
           AL.mult, AL.mult)
        VDR()
        OP("vector", "tensor_add", misc[:, M_SING + 3:M_SING + 4],
           misc[:, M_SING + 3:M_SING + 4], misc[:, M_SING + 4:M_SING + 5])
        mv = VDR(True)
        W("scalar", "ve", mv)
        OP("scalar", "activation", misc[:, M_SING + 5:M_SING + 6],
           misc[:, M_SING + 3:M_SING + 4], AF.Sqrt, 1e-5, 1.0, inc=(("ac", 1),))
        W("vector", "ac", mark("ac"))
        OP("vector", "reciprocal", misc[:, M_SING + 6:M_SING + 7],
           misc[:, M_SING + 5:M_SING + 6])
        VDR()
        OP("vector", "scalar_tensor_tensor", misc[:, M_SING + 7:M_SING + 8],
           misc[:, M_SING + 2:M_SING + 3], -1.0, misc[:, M_SING + 6:M_SING + 7],
           AL.mult, AL.mult)
        VDR()
        mvv = VDR(True)
        W("scalar", "ve", mvv)
        OP("scalar", "activation", atn[:, :], fpk[:, 0:1024], AF.Identity,
           misc[:, M_SING + 7:M_SING + 8], misc[:, M_SING + 6:M_SING + 7],
           inc=(("ac", 1),))
        W("vector", "ac", mark("ac"))
        OP("vector", "tensor_add", fpk[:, 1024:2048], xres[:, t * D:(t + 1) * D],
           atn[:, :], inc=(("ve", 1),))
        m_ffnev[t] = mark("ve")
        W("scalar", "ve", m_ffnev[t])
        OP("scalar", "dma_start", dout[t * 128:(t + 1) * 128, :], fpk[:, 1024:2048],
           inc=(("da", 16),))
        prev_out_dma = mark("da")
    W("scalar", "da", ctr["da"])

    if debug:
        OP("scalar", "copy", atn[:, 0:512], tbuf[:, 0:512])
        ADR(True)
        OP("scalar", "dma_start", dbg_gb2 := dbg_tb2[:, :] if False else dbg_tb2[:, :], atn[:, 0:512], inc=(("da", 16),))
        OP("scalar", "dma_start", dbg_misc[:, :], misc[:, :], inc=(("da", 16),))
        OP("scalar", "dma_start", dbg_gb[:, :], gb[:, :], inc=(("da", 16),))
        for t in range(NT):
            W("scalar", "da", ctr["da"])
            OP("scalar", "copy", atn[:, :], xres[:, t * D:(t + 1) * D])
            ADR(True)
            OP("scalar", "dma_start", dbg_xres[:, t * D:(t + 1) * D], atn[:, :],
               inc=(("da", 16),))
            W("scalar", "da", ctr["da"])
            OP("scalar", "copy", atn[:, :], ktm[:, t * D:(t + 1) * D])
            ADR(True)
            OP("scalar", "dma_start", dbg_ktm[:, t * D:(t + 1) * D], atn[:, :],
               inc=(("da", 16),))
        W("scalar", "da", ctr["da"])

    if trunc is not None:
        plan[:] = plan[:ckpt[trunc]]
        tctr = {k: 0 for k in ctr}
        for it in plan:
            if it[1] == "op":
                for s, n in it[5]:
                    tctr[s] += n
        for k in ctr:
            ctr[k] = tctr[k]
        plan.append(("sync", "w", "ds", ctr["ds"], None, None))
        plan.append(("scalar", "w", "gp", ctr["gp"], None, None))
        plan.append(("scalar", "w", "ve", ctr["ve"], None, None))
        plan.append(("scalar", "w", "ac", ctr["ac"], None, None))
        plan.append(("scalar", "w", "te", ctr["te"], None, None))
        plan.append(("scalar", "w", "cc", ctr["cc"], None, None))
        plan.append(("scalar", "w", "da", ctr["da"], None, None))
        plan.append(("scalar", "op", "dma_start",
                     (dout[0:128, :], misc[:, 0:1024]), {}, [("da", 16)]))
        ctr["da"] += 16
        plan.append(("scalar", "w", "da", ctr["da"], None, None))
        for e in ("tensor", "vector", "gpsimd"):
            plan.append((e, "op", "nop", (), {}, []))

    def runner(name):
        def f(eng):
            for it in plan:
                if it[0] != name:
                    continue
                if it[1] == "w":
                    eng.wait_ge(sems[it[2]], it[3])
                else:
                    _, _, meth, a, k, incs = it
                    ins = getattr(eng, meth)(*a, **k)
                    for s, n in incs:
                        ins = ins.then_inc(sems[s], n)
        return f

    with nc.Block() as blk:
        blk.sync(runner("sync"))
        blk.tensor(runner("tensor"))
        blk.vector(runner("vector"))
        blk.scalar(runner("scalar"))
        blk.gpsimd(runner("gpsimd"))
    return nc


_cache = {}


def kernel(L0_attn_x, L0_attn_Wq, L0_attn_Wk, L0_attn_Wv, L0_attn_Wo, L0_attn_Wf,
           L0_attn_delta, L0_attn_norm_weight, L0_ffn_weight_int8,
           L0_ffn_weight_scale, L0_ffn_bias):
    x = np.ascontiguousarray(np.asarray(L0_attn_x, dtype=np.float32).reshape(16384, 1024))
    wqt = np.ascontiguousarray(np.asarray(L0_attn_Wq, dtype=np.float32).T)
    wkt = np.ascontiguousarray(np.asarray(L0_attn_Wk, dtype=np.float32).T)
    wvt = np.ascontiguousarray(np.asarray(L0_attn_Wv, dtype=np.float32).T)
    wot = np.ascontiguousarray(np.asarray(L0_attn_Wo, dtype=np.float32).T)
    wft = np.ascontiguousarray(np.asarray(L0_attn_Wf, dtype=np.float32).T)
    delta = np.ascontiguousarray(np.asarray(L0_attn_delta, dtype=np.float32).reshape(16, 1))
    nw = np.ascontiguousarray(np.asarray(L0_attn_norm_weight, dtype=np.float32).reshape(8, 128).T)
    fw = np.ascontiguousarray(np.asarray(L0_ffn_weight_int8).T)
    fws = np.ascontiguousarray(np.asarray(L0_ffn_weight_scale, dtype=np.float32).reshape(1, 1024))
    fb = np.ascontiguousarray(np.asarray(L0_ffn_bias, dtype=np.float32).reshape(1, 1024))

    if "nc" not in _cache:
        _cache["nc"] = build_nc()
    nc = _cache["nc"]
    in_maps = []
    for s in range(NCORE):
        sm = np.zeros((16, 8), dtype=np.float32)
        sm[:, :s] = 1.0
        in_maps.append({
            "x": np.ascontiguousarray(x[s * TOK:(s + 1) * TOK, :]),
            "wqt": wqt, "wkt": wkt, "wvt": wvt, "wot": wot, "wft": wft,
            "delta": delta, "nw": nw, "fw": fw, "fws": fws, "fb": fb,
            "segmask": sm,
        })
    res = run_bass_kernel_spmd(nc, in_maps, core_ids=list(range(NCORE)))
    out = np.concatenate([np.asarray(r["out"]) for r in res.results], axis=0)
    return out.reshape(1, 16384, 1024).astype(np.float32)


# version 35
# speedup vs baseline: 1.0534x; 1.0098x over previous
import numpy as np
import concourse.bass as bass
import concourse.mybir as mybir
from concourse.bass_utils import run_bass_kernel_spmd

F32 = mybir.dt.float32
BF16 = mybir.dt.bfloat16
I8 = mybir.dt.int8
AF = mybir.ActivationFunctionType
AL = mybir.AluOpType
AX = mybir.AxisListType

NCORE = 8
TOK = 2048
NT = 16
D = 1024
H = 16
HD = 64
NDB = 8
RND = 1.5 * 2.0 ** 23

M_ST1, M_ST2, M_MX, M_MN = 0, 16, 32, 48
M_M3, M_VAR3, M_R3, M_MB3, M_RQ, M_MBQ = 64, 80, 96, 112, 128, 144
M_SING = 160
M_PAIRSC = 176
M_RCOL = 184
M_INVB = 188
M_NW = 192
M_HEAD = 200
M_WRS = 264
M_FWS = 776
M_XSROW = 1800
M_GMAX = 1930
M_SDROW = 2048
M_FB = 2176


def build_nc(trunc=None, debug=False):
    nc = bass.Bass("TRN2", target_bir_lowering=False,
                   detect_race_conditions=False,
                   dynamic_dma_scratch_size=512)

    dx = nc.dram_tensor("x", [TOK, D], F32, kind="ExternalInput")
    dwq = nc.dram_tensor("wqt", [D, D], F32, kind="ExternalInput")
    dwk = nc.dram_tensor("wkt", [D, D], F32, kind="ExternalInput")
    dwv = nc.dram_tensor("wvt", [D, D], F32, kind="ExternalInput")
    dwo = nc.dram_tensor("wot", [D, D], F32, kind="ExternalInput")
    dwf = nc.dram_tensor("wft", [D, H], F32, kind="ExternalInput")
    ddel = nc.dram_tensor("delta", [H, 1], F32, kind="ExternalInput")
    dnw = nc.dram_tensor("nw", [128, NDB], F32, kind="ExternalInput")
    dfw = nc.dram_tensor("fw", [D, D], I8, kind="ExternalInput")
    dfws = nc.dram_tensor("fws", [1, D], F32, kind="ExternalInput")
    dfb = nc.dram_tensor("fb", [1, D], F32, kind="ExternalInput")
    dsm = nc.dram_tensor("segmask", [H, NCORE], F32, kind="ExternalInput")
    dout = nc.dram_tensor("out", [TOK, D], F32, kind="ExternalOutput")
    agi = nc.dram_tensor("agi", [128, 520], F32)
    ago = nc.dram_tensor("ago", [128 * NCORE, 520], F32)
    agi2 = nc.dram_tensor("agi2", [1, 8], F32)
    ago2 = nc.dram_tensor("ago2", [NCORE, 8], F32)
    if debug:
        dbg_tb = nc.dram_tensor("dbg_tb", [128, 512], F32, kind="ExternalOutput")
        dbg_misc = nc.dram_tensor("dbg_misc", [128, 3200], F32, kind="ExternalOutput")
        dbg_xres = nc.dram_tensor("dbg_xres", [128, NT * D], F32, kind="ExternalOutput")
        dbg_ktm = nc.dram_tensor("dbg_ktm", [128, NT * D], F32, kind="ExternalOutput")
        dbg_gb = nc.dram_tensor("dbg_gb", [128, 520], F32, kind="ExternalOutput")
        dbg_tb2 = nc.dram_tensor("dbg_tb2", [128, 512], F32, kind="ExternalOutput")

    bfnp = mybir.dt.np(BF16)
    c_i128b = nc.inline_tensor(np.eye(128, dtype=np.float32).astype(bfnp), name="c_i128b")
    c_i128f = nc.inline_tensor(np.eye(128, dtype=np.float32), name="c_i128f")
    c_i16f = nc.inline_tensor(np.eye(16, dtype=np.float32), name="c_i16f")
    mk = (np.arange(128)[None, :] >= np.arange(128)[:, None])
    c_mask = nc.inline_tensor(mk.astype(bfnp), name="c_mask")
    c_onescol = nc.inline_tensor(np.ones((128, 1), np.float32).astype(bfnp), name="c_onescol")
    c_onesrow = nc.inline_tensor(np.ones((1, 128), np.float32), name="c_onesrow")
    p2 = np.zeros((2, 128), np.float32)
    p2[0, 0:64] = 1.0
    p2[1, 64:128] = 1.0
    c_pat2 = nc.inline_tensor(p2, name="c_pat2")
    p16 = np.zeros((16, 128), np.float32)
    for h in range(16):
        p16[h, 64 * (h % 2): 64 * (h % 2) + 64] = 1.0
    c_pat16 = nc.inline_tensor(p16, name="c_pat16")
    pfm = np.zeros((16, 512), np.float32)
    for h in range(16):
        pfm[h, 64 * (h // 2): 64 * (h // 2) + 64] = 1.0
    c_patf = nc.inline_tensor(pfm, name="c_patf")
    pf8 = np.zeros((16, 8), np.float32)
    for h in range(16):
        pf8[h, h // 2] = 1.0
    c_patf8 = nc.inline_tensor(pf8, name="c_patf8")
    c_vals = nc.inline_tensor(np.tile(np.array([[1e-5, 1e-6, RND, 0.0]], np.float32), (128, 1)), name="c_vals")
    i16p = np.zeros((128, 16), np.float32)
    i16p[0:16, 0:16] = np.eye(16)
    c_i16pad = nc.inline_tensor(i16p, name="c_i16pad")
    orp = np.zeros((128, 128), np.float32)
    orp[0, :] = 1.0
    c_onesrowpad = nc.inline_tensor(orp, name="c_onesrowpad")
    e0 = np.zeros((128, 1), np.float32)
    e0[0, 0] = 1.0
    c_e0col = nc.inline_tensor(e0, name="c_e0col")
    p16p = np.zeros((128, 128), np.float32)
    p16p[0:16, :] = p16
    c_pat16pad = nc.inline_tensor(p16p, name="c_pat16pad")

    xres = nc.alloc_sbuf_tensor("xres", [128, NT * D], BF16)
    xnfm = nc.alloc_sbuf_tensor("xnfm", [128, NT * D], BF16)
    ktm = nc.alloc_sbuf_tensor("ktm", [128, NT * D], BF16)
    vtm = nc.alloc_sbuf_tensor("vtm", [128, NT * D], BF16)
    wA = nc.alloc_sbuf_tensor("wA", [128, NDB * D], BF16)
    wB = nc.alloc_sbuf_tensor("wB", [128, NDB * D], BF16)
    wstg = nc.alloc_sbuf_tensor("wstg", [128, D], F32)
    fstg = nc.alloc_sbuf_tensor("fstg", [128, D], I8)
    fpk = nc.alloc_sbuf_tensor("fpk", [128, 2048], F32)
    expl = nc.alloc_sbuf_tensor("expl", [128, 512], F32)
    ztm = nc.alloc_sbuf_tensor("ztm", [128, 256], F32)
    qsil = nc.alloc_sbuf_tensor("qsil", [128, D], BF16)
    qfA = nc.alloc_sbuf_tensor("qfA", [128, D], BF16)
    qfB = nc.alloc_sbuf_tensor("qfB", [128, D], BF16)
    kf = nc.alloc_sbuf_tensor("kf", [128, D], BF16)
    att = nc.alloc_sbuf_tensor("att", [128, 2048], BF16)
    osc = nc.alloc_sbuf_tensor("osc", [128, D], BF16)
    atn = nc.alloc_sbuf_tensor("atn", [128, D], F32)
    wfb = nc.alloc_sbuf_tensor("wfb", [128, 128], BF16)
    sprime = nc.alloc_sbuf_tensor("sprime", [128, 512], BF16)
    tbuf = nc.alloc_sbuf_tensor("tbuf", [128, 512], F32)
    ttmp = nc.alloc_sbuf_tensor("ttmp", [128, 512], F32)
    gb = nc.alloc_sbuf_tensor("gb", [128, 520], F32)
    sbig = nc.alloc_sbuf_tensor("sbig", [128, D], BF16)
    bbig = nc.alloc_sbuf_tensor("bbig", [128, D], BF16)
    misc = nc.alloc_sbuf_tensor("misc", [128, 3200], F32)
    si128b = nc.alloc_sbuf_tensor("si128b", [128, 128], BF16)
    si128f = nc.alloc_sbuf_tensor("si128f", [128, 128], F32)
    si16f = nc.alloc_sbuf_tensor("si16f", [16, 16], F32)
    smask = nc.alloc_sbuf_tensor("smask", [128, 128], BF16)
    sonec = nc.alloc_sbuf_tensor("sonec", [128, 1], BF16)
    spatf = nc.alloc_sbuf_tensor("spatf", [16, 512], F32)
    spatf8 = nc.alloc_sbuf_tensor("spatf8", [16, 8], F32)
    si16p = nc.alloc_sbuf_tensor("si16p", [128, 16], F32)
    sorp = nc.alloc_sbuf_tensor("sorp", [128, 128], F32)
    se0c = nc.alloc_sbuf_tensor("se0c", [128, 1], F32)
    spat16p = nc.alloc_sbuf_tensor("spat16p", [128, 128], F32)
    svals = nc.alloc_sbuf_tensor("svals", [128, 4], F32)
    nc.const_aps.register(F32, 1e-5, svals[:, 0:1])
    nc.const_aps.register(F32, 1e-6, svals[:, 1:2])
    nc.const_aps.register(F32, RND, svals[:, 2:3])

    ps_s = nc.alloc_psum_tensor("ps_s", [128, 512], F32)
    ps_a = nc.alloc_psum_tensor("ps_a", [128, 1024], F32)
    ps_b = nc.alloc_psum_tensor("ps_b", [128, 1024], F32)
    ps_t = nc.alloc_psum_tensor("ps_t", [128, 512], F32)
    ps_m = nc.alloc_psum_tensor("ps_m", [128, 512], F32)
    ps_t2 = nc.alloc_psum_tensor("ps_t2", [128, 512], F32)

    plan = []
    ckpt = {}
    sems = {}
    ctr = {}
    for name in ("ds", "da", "cc", "te", "ve", "ac", "gp"):
        sems[name] = nc.alloc_semaphore(name)
        ctr[name] = 0

    def OP(eng, meth, *a, inc=(), **k):
        incs = []
        for s, n in inc:
            ctr[s] += n
            incs.append((s, n))
        plan.append((eng, "op", meth, a, k, incs))

    def W(eng, sem, val=None):
        plan.append((eng, "w", sem, val if val is not None else ctr[sem], None, None))

    def VDR(inc=False):
        OP("vector", "drain", inc=((("ve", 1),) if inc else ()))
        return ctr["ve"]

    def ADR(inc=False):
        OP("scalar", "drain", inc=((("ac", 1),) if inc else ()))
        return ctr["ac"]

    def GDR(inc=False):
        OP("gpsimd", "drain", inc=((("gp", 1),) if inc else ()))
        return ctr["gp"]

    def mark(sem):
        return ctr[sem]

    OP("vector", "memset", fpk[:, :], 0.0)
    VDR()
    OP("vector", "memset", fpk[96:112, :], 1.0)
    m_fpk = VDR(True)
    OP("vector", "memset", misc[:, :], 0.0)
    OP("vector", "memset", qfA[64:128, :], 0.0)
    OP("vector", "memset", qfB[0:64, :], 0.0)
    OP("vector", "memset", gb[:, 512:520], 0.0)
    m_zero = VDR(True)
    W("sync", "ve", m_zero)
    for dst, src in ((si128b, c_i128b), (si128f, c_i128f), (si16f, c_i16f),
                     (smask, c_mask), (sonec, c_onescol),
                     (spatf, c_patf),
                     (spatf8, c_patf8), (svals, c_vals), (si16p, c_i16pad),
                     (sorp, c_onesrowpad), (se0c, c_e0col), (spat16p, c_pat16pad)):
        OP("sync", "dma_start", dst[:, :], src.ap(), inc=(("ds", 16),))
    OP("sync", "dma_start", misc[0:16, M_HEAD + 35:M_HEAD + 36], ddel.ap(), inc=(("ds", 16),), allow_slow_non_contiguous=True)
    OP("sync", "dma_start", misc[:, M_NW:M_NW + 8], dnw.ap(), inc=(("ds", 16),))
    OP("sync", "dma_start", misc[0:1, M_FWS:M_FWS + D], dfws.ap(), inc=(("ds", 16),))
    OP("sync", "dma_start", misc[0:1, M_FB:M_FB + D], dfb.ap(), inc=(("ds", 16),))
    OP("sync", "dma_start", misc[0:16, M_HEAD + 36:M_HEAD + 44], dsm.ap(), inc=(("ds", 16),))
    m_consts = mark("ds")
    ckpt['consts'] = len(plan)
    OP("vector", "memset", misc[0:1, M_GMAX:M_GMAX + 8], 0.0)
    m_ones16 = VDR(True)

    xcast = []
    W("scalar", "ve", m_fpk)
    for t in range(NT):
        half = t % 2
        if t >= 2:
            W("scalar", "gp", xcast[t - 2])
        OP("scalar", "dma_start", fpk[:, half * 1024:(half + 1) * 1024],
           dx[t * 128:(t + 1) * 128, :], inc=(("da", 16),))
        W("gpsimd", "da", mark("da"))
        OP("gpsimd", "tensor_copy", xres[:, t * D:(t + 1) * D],
           fpk[:, half * 1024:(half + 1) * 1024])
        xcast.append(GDR(True))
    ckpt['xload'] = len(plan)

    gp_cast = []

    def load_w(dram, slot, scale_col=None, pre_wait=None):
        marks = []
        for db in range(NDB):
            if gp_cast:
                W("sync", gp_cast[-1][0], gp_cast[-1][1])
            OP("sync", "dma_start", wstg[:, :], dram[db * 128:(db + 1) * 128, :],
               inc=(("ds", 16),))
            W("vector", "ds", mark("ds"))
            if pre_wait is not None and db == 0:
                W("vector", pre_wait[0], pre_wait[1])
            if scale_col is None:
                OP("vector", "tensor_copy", slot[:, db * D:(db + 1) * D], wstg[:, :])
            else:
                OP("vector", "tensor_scalar_mul", slot[:, db * D:(db + 1) * D],
                   wstg[:, :], scale_col(db))
            m = VDR(True)
            gp_cast.append(("ve", m))
            marks.append(m)
        return marks

    ckpt['prewk'] = len(plan)
    wk_marks = load_w(dwk, wA)
    ckpt['wk'] = len(plan)
    wv_marks = load_w(dwv, wB)
    W("sync", gp_cast[-1][0], gp_cast[-1][1])
    for db in range(NDB):
        OP("sync", "dma_start", wstg[:, db * 16:(db + 1) * 16],
           dwf[db * 128:(db + 1) * 128, :], inc=(("ds", 16),))
    W("gpsimd", "ds", mark("ds"))
    OP("gpsimd", "tensor_copy", wfb[:, 0:128], wstg[:, 0:128])
    m_wf = GDR(True)
    gp_cast.append(("gp", m_wf))

    ckpt['g0'] = len(plan)
    for t in range(NT):
        W("vector", "gp", xcast[t])
        OP("vector", "tensor_reduce", misc[:, M_ST1 + t:M_ST1 + t + 1],
           xres[:, t * D:(t + 1) * D], AX.X, AL.add)
        W("scalar", "gp", xcast[t])
        OP("scalar", "activation", qsil[:, :], xres[:, t * D:(t + 1) * D], AF.Square,
           0.0, 1.0, accum_out=misc[:, M_ST2 + t:M_ST2 + t + 1])
    m_ln1s_v = VDR(True)
    m_ln1s_a = ADR(True)

    def ln_finalize(eps=1e-5):
        OP("vector", "tensor_scalar_mul", misc[:, M_M3:M_M3 + 16], misc[:, M_ST1:M_ST1 + 16], 1.0 / D)
        OP("vector", "tensor_scalar_mul", misc[:, M_VAR3:M_VAR3 + 16], misc[:, M_ST2:M_ST2 + 16], 1.0 / D)
        VDR()
        OP("vector", "scalar_tensor_tensor", misc[:, M_ST2:M_ST2 + 16], misc[:, M_M3:M_M3 + 16],
           -1.0, misc[:, M_M3:M_M3 + 16], AL.mult, AL.mult)
        VDR()
        OP("vector", "tensor_add", misc[:, M_VAR3:M_VAR3 + 16], misc[:, M_VAR3:M_VAR3 + 16],
           misc[:, M_ST2:M_ST2 + 16])
        mv = VDR(True)
        W("scalar", "ve", mv)
        OP("scalar", "activation", misc[:, M_ST1:M_ST1 + 16], misc[:, M_VAR3:M_VAR3 + 16],
           AF.Sqrt, eps, 1.0)
        ma = ADR(True)
        W("vector", "ac", ma)
        OP("vector", "reciprocal", misc[:, M_R3:M_R3 + 16], misc[:, M_ST1:M_ST1 + 16])
        VDR()
        OP("vector", "scalar_tensor_tensor", misc[:, M_MB3:M_MB3 + 16], misc[:, M_M3:M_M3 + 16],
           -1.0, misc[:, M_R3:M_R3 + 16], AL.mult, AL.mult)
        return VDR(True)

    W("vector", "ac", m_ln1s_a)
    m_ln1f = ln_finalize()

    xn_marks = []
    for t in range(NT):
        W("scalar", "ve", m_ln1f)
        if t > 0:
            W("scalar", "ve", xn_marks[-1])
        OP("scalar", "activation", kf[:, :], xres[:, t * D:(t + 1) * D], AF.Identity,
           misc[:, M_MB3 + t:M_MB3 + t + 1], misc[:, M_R3 + t:M_R3 + t + 1],
           inc=(("ac", 1),))
        m_xn = mark("ac")
        for rnd in range(2):
            W("tensor", "ac", m_xn)
            if t > 0 or rnd > 0:
                W("tensor", "ve", ctr["ve"])
            if t == 0 and rnd == 0:
                W("tensor", "ds", m_consts)
            for j in range(4):
                db = rnd * 4 + j
                OP("tensor", "matmul", ps_t[:, j * 128:(j + 1) * 128],
                   kf[:, db * 128:(db + 1) * 128], si128b[:, :],
                   start=(j == 0), stop=(j == 3),
                   inc=((("te", 1),) if j == 3 else ()))
            W("vector", "te", mark("te"))
            out3 = bass.AP(xnfm, rnd * 4 * 2048 + t * 128,
                           [[NT * D, 128], [2048, 4], [1, 128]])
            in3 = ps_t[:, :].rearrange("p (b i) -> p b i", b=4)
            OP("vector", "tensor_copy", out3, in3, inc=(("ve", 1),))
        xn_marks.append(mark("ve"))

    ckpt['ln1'] = len(plan)
    W("tensor", "gp", m_wf)
    W("tensor", "ve", xn_marks[-1])
    for t in range(NT):
        for db in range(NDB):
            last = (t == NT - 1 and db == NDB - 1)
            OP("tensor", "matmul", ps_m[:, t * 16:(t + 1) * 16],
               xnfm[:, db * 2048 + t * 128: db * 2048 + (t + 1) * 128],
               wfb[:, db * 16:(db + 1) * 16],
               start=(t == 0 and db == 0), stop=last,
               inc=((("te", 1),) if last else ()))
    m_fmm = mark("te")
    W("vector", "te", m_fmm)
    OP("vector", "tensor_copy", ztm[:, 0:256], ps_m[:, 0:256], inc=(("ve", 1),))
    m_zev = mark("ve")

    for g in range(4):
        W("tensor", "ve", m_zev if g == 0 else ctr["ve"])
        for j in range(4):
            t = g * 4 + j
            OP("tensor", "matmul", ps_t[0:16, j * 128:(j + 1) * 128],
               ztm[:, t * 16:(t + 1) * 16], si128f[:, :],
               start=(j == 0), stop=(j == 3),
               inc=((("te", 1),) if j == 3 else ()))
        W("vector", "te", mark("te"))
        OP("vector", "tensor_copy", fpk[32:48, g * 512:(g + 1) * 512],
           ps_t[0:16, 0:512], inc=(("ve", 1),))
    m_zfm = mark("ve")
    W("scalar", "ve", m_zfm)
    W("scalar", "ds", m_consts)
    OP("scalar", "activation", fpk[64:80, :], fpk[32:48, :], AF.Sigmoid,
       misc[0:16, M_HEAD + 35:M_HEAD + 36], 1.0)
    ADR()
    OP("scalar", "activation", fpk[32:48, :], fpk[64:80, :], AF.Ln, 0.0, 1.0)
    m_lsig = ADR(True)
    W("vector", "ac", m_lsig)
    W("vector", "ve", m_ones16)
    OP("vector", "tensor_tensor_scan", fpk[0:16, :], fpk[32:48, :], fpk[32:48, :],
       0.0, AL.add, AL.bypass)
    VDR()
    OP("vector", "tensor_copy", misc[0:16, M_HEAD:M_HEAD + 1], fpk[0:16, 2047:2048])
    m_lfm = VDR(True)
    W("tensor", "ve", m_lfm)
    for c in range(NT):
        OP("tensor", "matmul", ps_m[:, c * 16:(c + 1) * 16],
           fpk[:, c * 128:(c + 1) * 128], si16p[:, :],
           start=(c == 0), stop=(c == NT - 1),
           inc=((("te", 1),) if c == NT - 1 else ()))
    m_ltr = mark("te")
    W("scalar", "te", m_ltr)
    OP("scalar", "activation", expl[:, 0:256], ps_m[:, 0:256], AF.Exp, 0.0, 1.0)
    ADR()
    OP("scalar", "activation", expl[:, 256:512], ps_m[:, 0:256], AF.Exp, 0.0, -1.0)
    m_expl = ADR(True)

    ckpt['fL'] = len(plan)
    kv_ev_a = None
    kv_ev_b = None
    ktm_marks = []
    for c in range(NT):
        if kv_ev_a is not None:
            W("tensor", "ac", kv_ev_a)
            W("tensor", "ve", ktm_marks[-1])
        else:
            W("tensor", "ve", wk_marks[-1])
            W("tensor", "ve", wv_marks[-1])
        for db in range(NDB):
            lhs = xnfm[:, db * 2048 + c * 128: db * 2048 + (c + 1) * 128]
            for jh in range(2):
                last = (db == NDB - 1 and jh == 1)
                OP("tensor", "matmul", ps_a[:, jh * 512:(jh + 1) * 512], lhs,
                   wA[:, db * D + jh * 512: db * D + (jh + 1) * 512],
                   start=(db == 0), stop=(db == NDB - 1),
                   inc=((("te", 1),) if last else ()))
        m_kmm = mark("te")
        if kv_ev_b is not None:
            W("tensor", "ac", kv_ev_b)
        for db in range(NDB):
            lhs = xnfm[:, db * 2048 + c * 128: db * 2048 + (c + 1) * 128]
            for jh in range(2):
                last = (db == NDB - 1 and jh == 1)
                OP("tensor", "matmul", ps_b[:, jh * 512:(jh + 1) * 512], lhs,
                   wB[:, db * D + jh * 512: db * D + (jh + 1) * 512],
                   start=(db == 0), stop=(db == NDB - 1),
                   inc=((("te", 1),) if last else ()))
        m_vmm = mark("te")
        W("scalar", "te", m_kmm)
        if c > 0:
            W("scalar", "ve", ktm_marks[-1])
        if c == 0:
            W("scalar", "ac", m_expl)
        OP("scalar", "activation", qsil[:, :], ps_a[:, :], AF.Silu, 0.0, 1.0,
           inc=(("ac", 1),))
        kv_ev_a = mark("ac")
        W("vector", "ac", kv_ev_a)
        o4 = ktm[:, c * D:(c + 1) * D].rearrange("p (b h d) -> p b h d", b=NDB, h=2)
        i4 = qsil[:, :].rearrange("p (b h d) -> p b h d", b=NDB, h=2)
        e4 = expl[:, 256 + c * 16:256 + (c + 1) * 16].rearrange(
            "p (b h d) -> p b h d", b=NDB, h=2, d=1)
        e4b, i4b = bass.broadcast_tensor_aps(e4, i4)
        OP("vector", "tensor_tensor", o4, i4b, e4b, AL.mult, inc=(("ve", 1),))
        ktm_marks.append(mark("ve"))
        W("scalar", "te", m_vmm)
        OP("scalar", "copy", vtm[:, c * D:(c + 1) * D], ps_b[:, :],
           inc=(("ac", 1),))
        kv_ev_b = mark("ac")
        W("tensor", "ve", ktm_marks[-1])
        W("tensor", "ac", kv_ev_b)
        for p in range(NDB):
            for b in range(2):
                h = 2 * p + b
                last = (p == NDB - 1 and b == 1)
                OP("tensor", "matmul", ps_s[b * 64:(b + 1) * 64, p * 64:(p + 1) * 64],
                   ktm[:, c * D + h * 64: c * D + (h + 1) * 64],
                   vtm[:, c * D + h * 64: c * D + (h + 1) * 64],
                   start=(c == 0 and p == 0), stop=(c == NT - 1 and p == NDB - 1),
                   tile_position=(0, b * 64), skip_group_check=True,
                   inc=((("te", 1),) if last else ()))
        m_u = mark("te")
    m_sweep1 = m_u

    ckpt['sweep1'] = len(plan)
    wq_marks = load_w(dwq, wA, pre_wait=("te", m_sweep1))
    W("scalar", "ve", m_lfm)
    OP("scalar", "activation", misc[0:16, M_HEAD + 1:M_HEAD + 2],
       misc[0:16, M_HEAD:M_HEAD + 1], AF.Exp, 0.0, 1.0)
    m_lamE = ADR(True)
    W("vector", "ac", m_lamE)
    W("vector", "ds", m_consts)
    OP("vector", "tensor_scalar_mul", misc[0:16, M_HEAD + 45:M_HEAD + 53],
       spatf8[:, :], misc[0:16, M_HEAD + 1:M_HEAD + 2])
    m_pfe = VDR(True)
    W("tensor", "ve", m_pfe)
    OP("tensor", "matmul", ps_m[:, 0:8], spat16p[:, :],
       misc[:, M_HEAD + 45:M_HEAD + 53], start=True, stop=True,
       inc=(("te", 1),))
    W("vector", "te", mark("te"))
    OP("vector", "tensor_copy", misc[:, M_PAIRSC:M_PAIRSC + 8], ps_m[:, 0:8])
    m_pairsc = VDR(True)
    W("scalar", "ve", m_pairsc)
    for p in range(NDB):
        OP("scalar", "activation", gb[:, p * 64:(p + 1) * 64],
           ps_s[:, p * 64:(p + 1) * 64], AF.Identity, 0.0,
           misc[:, M_PAIRSC + p:M_PAIRSC + p + 1])
    m_bsend = ADR(True)
    W("vector", "ac", m_bsend)
    W("vector", "ve", m_ones16)
    OP("vector", "tensor_copy", gb[0:16, 512:513], misc[0:16, M_HEAD:M_HEAD + 1],
       inc=(("ve", 1),))
    m_gblam = mark("ve")
    W("sync", "ve", m_gblam)
    OP("sync", "dma_start", agi[0:128, 0:520], gb[:, 0:520], inc=(("ds", 16),))
    m_agdma = mark("ds")
    W("gpsimd", "ds", m_agdma)
    OP("gpsimd", "collective_compute", "AllGather", AL.bypass,
       replica_groups=[list(range(NCORE))], ins=[agi.ap().opt()],
       outs=[ago.ap().opt()], inc=(("cc", 1),))
    m_cc1 = mark("cc")
    ckpt['exch'] = len(plan)
    W("sync", "cc", m_cc1)
    for s in range(NCORE):
        OP("sync", "dma_start", misc[0:16, M_HEAD + 2 + s:M_HEAD + 3 + s],
           ago[s * 128:s * 128 + 16, 512:513], inc=(("ds", 16),),
           allow_slow_non_contiguous=True)
    m_lamall = mark("ds")
    q_marks = []
    ev_a = kv_ev_a
    for t in range(NT):
        if t == 0:
            W("tensor", "ve", wq_marks[-1])
        W("tensor", "ac", ev_a)
        for db in range(NDB):
            lhs = xnfm[:, db * 2048 + t * 128: db * 2048 + (t + 1) * 128]
            for jh in range(2):
                last = (db == NDB - 1 and jh == 1)
                OP("tensor", "matmul", ps_a[:, jh * 512:(jh + 1) * 512], lhs,
                   wA[:, db * D + jh * 512: db * D + (jh + 1) * 512],
                   start=(db == 0), stop=(db == NDB - 1),
                   inc=((("te", 1),) if last else ()))
        m_qmm = mark("te")
        W("scalar", "te", m_qmm)
        if q_marks:
            W("scalar", "ve", q_marks[-1])
        OP("scalar", "activation", qsil[:, :], ps_a[:, :], AF.Silu, 0.0, 1.0,
           inc=(("ac", 1),))
        ev_a = mark("ac")
        W("vector", "ac", ev_a)
        out4 = bass.AP(xnfm, t * 128, [[NT * D, 128], [2048, NDB], [64, 2], [1, 64]])
        i4 = qsil[:, :].rearrange("p (b h d) -> p b h d", b=NDB, h=2)
        e4 = expl[:, t * 16:(t + 1) * 16].rearrange("p (b h d) -> p b h d",
                                                    b=NDB, h=2, d=1)
        e4b, i4b = bass.broadcast_tensor_aps(e4, i4)
        OP("vector", "tensor_tensor", out4, i4b, e4b, AL.mult, inc=(("ve", 1),))
        q_marks.append(mark("ve"))
    m_qall = mark("ve")

    ckpt['bulkq'] = len(plan)
    W("vector", "ds", m_lamall)
    W("vector", "ve", m_ones16)
    OP("vector", "tensor_tensor_scan", misc[0:16, M_HEAD + 10:M_HEAD + 18],
       misc[0:16, M_HEAD + 2:M_HEAD + 10], misc[0:16, M_HEAD + 2:M_HEAD + 10],
       0.0, AL.add, AL.bypass)
    VDR()
    OP("vector", "tensor_mul", misc[0:16, M_HEAD + 19:M_HEAD + 27],
       misc[0:16, M_HEAD + 2:M_HEAD + 10], misc[0:16, M_HEAD + 36:M_HEAD + 44])
    VDR()
    OP("vector", "tensor_reduce", misc[0:16, M_HEAD + 18:M_HEAD + 19],
       misc[0:16, M_HEAD + 19:M_HEAD + 27], AX.X, AL.add)
    VDR()
    OP("vector", "tensor_scalar", misc[0:16, M_HEAD + 45:M_HEAD + 53],
       misc[0:16, M_HEAD + 10:M_HEAD + 18], -1.0,
       misc[0:16, M_HEAD + 18:M_HEAD + 19], AL.mult, AL.add)
    VDR()
    OP("vector", "tensor_scalar", misc[0:16, M_HEAD + 45:M_HEAD + 53],
       misc[0:16, M_HEAD + 45:M_HEAD + 53], 0.0, None, AL.min)
    m_wm1 = VDR(True)
    W("scalar", "ve", m_wm1)
    OP("scalar", "activation", misc[0:16, M_HEAD + 19:M_HEAD + 27],
       misc[0:16, M_HEAD + 45:M_HEAD + 53], AF.Exp, 0.0, 1.0)
    m_wraw = ADR(True)
    W("vector", "ac", m_wraw)
    OP("vector", "tensor_mul", misc[0:16, M_HEAD + 27:M_HEAD + 35],
       misc[0:16, M_HEAD + 19:M_HEAD + 27], misc[0:16, M_HEAD + 36:M_HEAD + 44])
    m_wmat = VDR(True)

    t_marks = []
    for s in range(NCORE):
        if t_marks:
            W("sync", "ve", t_marks[-1])
        OP("sync", "dma_start", gb[:, 0:512], ago[s * 128:(s + 1) * 128, 0:512],
           inc=(("ds", 16),))
        m_gbl = mark("ds")
        if s == 0:
            W("vector", "ve", m_wmat)
        OP("vector", "tensor_scalar_mul", misc[0:16, M_WRS:M_WRS + 512],
           spatf[:, :], misc[0:16, M_HEAD + 27 + s:M_HEAD + 28 + s])
        m_wrs = VDR(True)
        W("tensor", "ve", m_wrs)
        OP("tensor", "matmul", ps_m[:, 0:512], spat16p[:, :],
           misc[:, M_WRS:M_WRS + 512], start=True, stop=True,
           inc=(("te", 1),))
        m_wtile = mark("te")
        W("vector", "te", m_wtile)
        W("vector", "ds", m_gbl)
        if s == 0:
            OP("vector", "tensor_mul", tbuf[:, :], gb[:, 0:512], ps_m[:, 0:512])
        else:
            OP("vector", "tensor_mul", ttmp[:, :], gb[:, 0:512], ps_m[:, 0:512])
            VDR()
            OP("vector", "tensor_add", tbuf[:, :], tbuf[:, :], ttmp[:, :])
        t_marks.append(VDR(True))
    m_T = t_marks[-1]
    if debug:
        W("scalar", "ve", m_T)
        OP("scalar", "dma_start", dbg_tb[:, :], tbuf[:, :], inc=(("da", 16),))
    W("vector", "ve", m_T)
    OP("vector", "tensor_copy", sprime[:, :], tbuf[:, :], inc=(("ve", 1),))
    m_sp0 = mark("ve")

    wo_marks = load_w(dwo, wB, scale_col=lambda db: misc[:, M_NW + db:M_NW + db + 1],
                      pre_wait=("te", m_sweep1))

    ckpt['T'] = len(plan)
    m_attnev = None
    m_psbfree = None
    m_oev = None
    tr_bank_ev = [None, None]
    sp_mark = m_sp0
    m_rcol_prev = None
    res_marks = []
    for c in range(NT):
        ckpt[f's2c{c}'] = len(plan)
        trk = 0
        for which in ("q", "k"):
            for rnd in range(2):
                pst_ = ps_t if trk % 2 == 0 else ps_t2
                W("tensor", "ve", q_marks[c] if which == "q" else ktm_marks[c])
                bank_id = trk % 2
                if tr_bank_ev[bank_id] is not None:
                    W("tensor", "ac", tr_bank_ev[bank_id])
                for j in range(4):
                    db = rnd * 4 + j
                    if which == "q":
                        lhs = bass.AP(xnfm, db * 2048 + c * 128,
                                      [[NT * D, 128], [1, 128]])
                    else:
                        lhs = ktm[:, c * D + db * 128: c * D + (db + 1) * 128]
                    OP("tensor", "matmul", pst_[:, j * 128:(j + 1) * 128], lhs,
                       si128b[:, :], start=(j == 0), stop=(j == 3),
                       inc=((("te", 1),) if j == 3 else ()))
                W("scalar", "te", mark("te"))
                if which == "q":
                    OP("scalar", "copy", qfA[0:64, rnd * 512:(rnd + 1) * 512],
                       pst_[0:64, :])
                    OP("scalar", "copy", qfB[64:128, rnd * 512:(rnd + 1) * 512],
                       pst_[64:128, :], inc=(("ac", 1),))
                else:
                    OP("scalar", "copy", kf[:, rnd * 512:(rnd + 1) * 512],
                       pst_[:, :], inc=(("ac", 1),))
                tr_bank_ev[bank_id] = mark("ac")
                trk += 1
        m_tr = mark("ac")
        ckpt[f's2tr{c}'] = len(plan)
        att_marks = []
        for g in range(4):
            W("tensor", "ac", m_tr)
            if g >= 2:
                W("tensor", "ve", att_marks[g - 2])
            elif m_psbfree is not None:
                W("tensor", "ve", m_psbfree)
            half = g % 2
            for j in range(4):
                h = g * 4 + j
                p, b = h // 2, h % 2
                qsrc = qfA if b == 0 else qfB
                OP("tensor", "matmul",
                   ps_b[:, half * 512 + j * 128: half * 512 + (j + 1) * 128],
                   kf[:, p * 128:(p + 1) * 128],
                   qsrc[:, p * 128:(p + 1) * 128],
                   start=True, stop=True,
                   inc=((("te", 1),) if j == 3 else ()))
            W("vector", "te", mark("te"))
            i3 = ps_b[:, half * 512:(half + 1) * 512].rearrange("p (g i) -> p g i", g=4)
            m3 = smask[:, :].rearrange("p (g i) -> p g i", g=1)
            m3b, i3b = bass.broadcast_tensor_aps(m3, i3)
            o3 = att[:, g * 512:(g + 1) * 512].rearrange("p (g i) -> p g i", g=4)
            OP("vector", "tensor_tensor", o3, i3b, m3b, AL.mult, inc=(("ve", 1),))
            att_marks.append(mark("ve"))
        ckpt[f's2at{c}'] = len(plan)
        if m_oev is not None:
            W("tensor", "ac", m_oev)
        W("tensor", "ve", att_marks[-1])
        W("tensor", "ve", sp_mark)
        if c == 0:
            W("tensor", "ac", m_bsend)
        for p in range(NDB):
            for b in range(2):
                h = 2 * p + b
                OP("tensor", "matmul",
                   ps_a[b * 64:(b + 1) * 64, p * 128:(p + 1) * 128],
                   vtm[:, c * D + h * 64: c * D + (h + 1) * 64],
                   att[:, h * 128:(h + 1) * 128],
                   start=(p % 4 == 0), stop=False, tile_position=(0, b * 64),
                   skip_group_check=True)
            for b in range(2):
                qsrc = qfA if b == 0 else qfB
                last = (p == NDB - 1 and b == 1)
                OP("tensor", "matmul",
                   ps_a[b * 64:(b + 1) * 64, p * 128:(p + 1) * 128],
                   sprime[:, p * 64:(p + 1) * 64],
                   qsrc[:, p * 128:(p + 1) * 128],
                   start=False, stop=(p % 4 == 3), tile_position=(0, b * 64),
                   skip_group_check=True,
                   inc=((("te", 1),) if last else ()))
        m_omm = mark("te")
        if c > 0:
            W("tensor", "ve", sp_mark)
        for p in range(NDB):
            for b in range(2):
                h = 2 * p + b
                last = (p == NDB - 1 and b == 1)
                OP("tensor", "matmul", ps_s[b * 64:(b + 1) * 64, p * 64:(p + 1) * 64],
                   ktm[:, c * D + h * 64: c * D + (h + 1) * 64],
                   vtm[:, c * D + h * 64: c * D + (h + 1) * 64],
                   start=(c == 0 and p == 0), stop=(c == NT - 1 and p == NDB - 1),
                   tile_position=(0, b * 64), skip_group_check=True,
                   inc=((("te", 1),) if last else ()))
        m_umm = mark("te")
        W("vector", "te", m_umm)
        OP("vector", "tensor_add", sprime[:, :], ps_s[:, :], tbuf[:, :],
           inc=(("ve", 1),))
        sp_mark = mark("ve")
        ckpt[f's2u{c}'] = len(plan)
        W("scalar", "te", m_omm)
        OP("scalar", "copy", osc[:, :], ps_a[:, :], inc=(("ac", 1),))
        m_oev = mark("ac")
        OP("scalar", "activation", qsil[:, :], osc[:, :], AF.Square, 0.0, 1.0,
           inc=(("ac", 1),))
        m_osq = mark("ac")
        ckpt[f's2r{c}'] = len(plan)
        if c == 0:
            W("tensor", "ve", wo_marks[-1])
        W("tensor", "ve", att_marks[-1])
        if m_psbfree is not None:
            W("tensor", "ve", m_psbfree)
        W("tensor", "ac", m_oev)
        for db in range(NDB):
            for jh in range(2):
                last = (db == NDB - 1 and jh == 1)
                OP("tensor", "matmul", ps_b[:, jh * 512:(jh + 1) * 512],
                   osc[:, db * 128:(db + 1) * 128],
                   wB[:, db * D + jh * 512: db * D + (jh + 1) * 512],
                   start=(db == 0), stop=(db == NDB - 1),
                   inc=((("te", 1),) if last else ()))
        m_womm = mark("te")
        W("tensor", "ac", m_osq)
        if m_rcol_prev is not None:
            W("tensor", "ac", m_rcol_prev[0])
        for db in range(NDB):
            OP("tensor", "matmul", ps_m[:, 0:1],
               qsil[:, db * 128:(db + 1) * 128], sonec[:, :],
               start=(db == 0), stop=(db == NDB - 1),
               inc=((("te", 1),) if db == NDB - 1 else ()))
        m_ssum = mark("te")
        W("scalar", "te", m_ssum)
        OP("scalar", "activation", misc[:, M_RCOL + 1:M_RCOL + 2],
           ps_m[:, 0:1], AF.Sqrt, 1e-6, 1.0 / D, inc=(("ac", 1),))
        m_sd = mark("ac")
        m_rcol_prev = (m_sd, None)
        W("vector", "ac", m_sd)
        OP("vector", "reciprocal", misc[:, M_RCOL:M_RCOL + 1],
           misc[:, M_RCOL + 1:M_RCOL + 2])
        m_rcol = VDR(True)
        W("vector", "te", m_womm)
        if res_marks:
            W("vector", "ve", res_marks[-1])
        OP("vector", "tensor_scalar", atn[:, :], ps_b[:, :], 1.0, 0.0,
           AL.mult, AL.add, accum_out=misc[:, M_SING:M_SING + 1],
           inc=(("ve", 1),))
        m_psbfree = mark("ve")
        VDR()
        W("scalar", "te", m_ssum)
        W("scalar", "ve", m_psbfree)
        OP("scalar", "activation", qsil[:, :], atn[:, :], AF.Square, 0.0, 1.0,
           accum_out=misc[:, M_SING + 1:M_SING + 2], inc=(("ac", 1),))
        W("vector", "ac", mark("ac"))
        OP("vector", "tensor_scalar_mul", misc[:, M_SING + 2:M_SING + 3],
           misc[:, M_SING:M_SING + 1], 1.0 / D)
        OP("vector", "tensor_scalar_mul", misc[:, M_SING + 3:M_SING + 4],
           misc[:, M_SING + 1:M_SING + 2], 1.0 / D)
        VDR()
        OP("vector", "scalar_tensor_tensor", misc[:, M_SING + 4:M_SING + 5],
           misc[:, M_SING + 2:M_SING + 3], -1.0, misc[:, M_SING + 2:M_SING + 3],
           AL.mult, AL.mult)
        VDR()
        OP("vector", "tensor_add", misc[:, M_SING + 3:M_SING + 4],
           misc[:, M_SING + 3:M_SING + 4], misc[:, M_SING + 4:M_SING + 5])
        VDR()
        OP("vector", "scalar_tensor_tensor", misc[:, M_SING + 4:M_SING + 5],
           misc[:, M_RCOL:M_RCOL + 1], 1.0, misc[:, M_RCOL:M_RCOL + 1],
           AL.mult, AL.mult)
        VDR()
        OP("vector", "tensor_mul", misc[:, M_SING + 3:M_SING + 4],
           misc[:, M_SING + 3:M_SING + 4], misc[:, M_SING + 4:M_SING + 5])
        mv = VDR(True)
        W("scalar", "ve", mv)
        OP("scalar", "activation", misc[:, M_SING + 5:M_SING + 6],
           misc[:, M_SING + 3:M_SING + 4], AF.Sqrt, 1e-5, 1.0, inc=(("ac", 1),))
        W("vector", "ac", mark("ac"))
        OP("vector", "reciprocal", misc[:, M_SING + 6:M_SING + 7],
           misc[:, M_SING + 5:M_SING + 6])
        VDR()
        OP("vector", "tensor_mul", misc[:, M_SING + 6:M_SING + 7],
           misc[:, M_SING + 6:M_SING + 7], misc[:, M_RCOL:M_RCOL + 1])
        VDR()
        OP("vector", "scalar_tensor_tensor", misc[:, M_SING + 7:M_SING + 8],
           misc[:, M_SING + 2:M_SING + 3], -1.0, misc[:, M_SING + 6:M_SING + 7],
           AL.mult, AL.mult)
        VDR()
        mvv = VDR(True)
        W("scalar", "ve", mvv)
        OP("scalar", "activation", fpk[:, 0:1024], atn[:, :], AF.Identity,
           misc[:, M_SING + 7:M_SING + 8], misc[:, M_SING + 6:M_SING + 7],
           inc=(("ac", 1),))
        W("vector", "ac", mark("ac"))
        OP("vector", "engine_nop", inc=(("ve", 1),))
        m_attnev = mark("ve")
        ckpt[f's2w{c}'] = len(plan)
        OP("vector", "tensor_add", xres[:, c * D:(c + 1) * D],
           xres[:, c * D:(c + 1) * D], fpk[:, 0:1024])
        res_marks.append(VDR(True))

    ckpt['sweep2'] = len(plan)
    fw_marks = []
    for db in range(NDB):
        if fw_marks:
            W("sync", "gp", fw_marks[-1])
        OP("sync", "dma_start", fstg[:, :], dfw[db * 128:(db + 1) * 128, :],
           inc=(("ds", 16),))
        W("gpsimd", "ds", mark("ds"))
        if db == 0:
            W("gpsimd", "ve", m_qall)
        OP("gpsimd", "tensor_copy", wA[:, db * D:(db + 1) * D], fstg[:, :])
        fw_marks.append(GDR(True))

    for t in range(NT):
        OP("vector", "tensor_reduce", misc[:, M_ST1 + t:M_ST1 + t + 1],
           xres[:, t * D:(t + 1) * D], AX.X, AL.add)
        OP("vector", "tensor_reduce", misc[:, M_MX + t:M_MX + t + 1],
           xres[:, t * D:(t + 1) * D], AX.X, AL.max)
        OP("vector", "tensor_reduce", misc[:, M_MN + t:M_MN + t + 1],
           xres[:, t * D:(t + 1) * D], AX.X, AL.min)
        W("scalar", "ve", res_marks[t])
        W("scalar", "ve", m_attnev)
        OP("scalar", "activation", qsil[:, :], xres[:, t * D:(t + 1) * D], AF.Square,
           0.0, 1.0, accum_out=misc[:, M_ST2 + t:M_ST2 + t + 1])
    VDR(True)
    m2a = ADR(True)
    W("vector", "ac", m2a)
    m_ln2f = ln_finalize()
    W("vector", "ve", m_ln2f)
    OP("vector", "tensor_sub", misc[:, M_ST1:M_ST1 + 16], misc[:, M_MX:M_MX + 16],
       misc[:, M_M3:M_M3 + 16])
    OP("vector", "tensor_sub", misc[:, M_ST2:M_ST2 + 16], misc[:, M_M3:M_M3 + 16],
       misc[:, M_MN:M_MN + 16])
    VDR()
    OP("vector", "tensor_max", misc[:, M_ST1:M_ST1 + 16], misc[:, M_ST1:M_ST1 + 16],
       misc[:, M_ST2:M_ST2 + 16])
    VDR()
    OP("vector", "tensor_mul", misc[:, M_ST1:M_ST1 + 16], misc[:, M_ST1:M_ST1 + 16],
       misc[:, M_R3:M_R3 + 16])
    VDR()
    OP("vector", "tensor_reduce", misc[:, M_ST2:M_ST2 + 1], misc[:, M_ST1:M_ST1 + 16],
       AX.X, AL.max)
    m_rowabs = VDR(True)
    W("tensor", "ve", m_rowabs)
    OP("tensor", "matmul", ps_m[0:1, 256:384], misc[:, M_ST2:M_ST2 + 1],
       si128f[:, :], start=True, stop=True, inc=(("te", 1),))
    W("vector", "te", mark("te"))
    OP("vector", "tensor_reduce", misc[0:1, M_GMAX:M_GMAX + 1],
       ps_m[0:1, 256:384], AX.X, AL.max)
    m_locmax = VDR(True)
    W("sync", "ve", m_locmax)
    W("sync", "ve", m_ones16)
    OP("sync", "dma_start", agi2[0:1, 0:8], misc[0:1, M_GMAX:M_GMAX + 8],
       inc=(("ds", 16),))
    W("gpsimd", "ds", mark("ds"))
    OP("gpsimd", "collective_compute", "AllGather", AL.bypass,
       replica_groups=[list(range(NCORE))], ins=[agi2.ap().opt()],
       outs=[ago2.ap().opt()], inc=(("cc", 1),))
    m_cc2 = mark("cc")
    W("sync", "cc", m_cc2)
    OP("sync", "dma_start", misc[0:8, M_HEAD + 44:M_HEAD + 45], ago2[0:8, 0:1],
       inc=(("ds", 16),), allow_slow_non_contiguous=True)
    m_gm8 = mark("ds")
    W("tensor", "ds", m_gm8)
    OP("tensor", "matmul", ps_m[0:1, 384:392], misc[0:8, M_HEAD + 44:M_HEAD + 45],
       si16f[0:8, 0:8], start=True, stop=True, inc=(("te", 1),))
    W("vector", "te", mark("te"))
    OP("vector", "tensor_reduce", misc[0:1, M_GMAX + 1:M_GMAX + 2],
       ps_m[0:1, 384:392], AX.X, AL.max)
    VDR()
    OP("vector", "tensor_scalar", misc[0:1, M_GMAX + 2:M_GMAX + 3],
       misc[0:1, M_GMAX + 1:M_GMAX + 2], 1.0 / 127.0, 1e-5, AL.mult, AL.max)
    VDR()
    OP("vector", "reciprocal", misc[0:1, M_GMAX + 3:M_GMAX + 4],
       misc[0:1, M_GMAX + 2:M_GMAX + 3])
    m_xs = VDR(True)
    W("tensor", "ve", m_xs)
    OP("tensor", "matmul", ps_m[:, 392:393], sorp[:, :],
       misc[:, M_GMAX + 3:M_GMAX + 4], start=True, stop=True)
    OP("tensor", "matmul", ps_m[0:1, 136:264],
       misc[:, M_GMAX + 2:M_GMAX + 3], sorp[:, :], start=True, stop=True,
       inc=(("te", 1),))
    W("vector", "te", mark("te"))
    OP("vector", "tensor_copy", misc[:, M_INVB:M_INVB + 1], ps_m[:, 392:393])
    VDR()
    OP("vector", "tensor_copy", misc[0:1, M_XSROW:M_XSROW + 128], ps_m[0:1, 136:264])
    VDR()
    OP("vector", "tensor_scalar_mul", misc[:, M_RQ:M_RQ + 16],
       misc[:, M_R3:M_R3 + 16], misc[:, M_INVB:M_INVB + 1])
    OP("vector", "tensor_scalar_mul", misc[:, M_MBQ:M_MBQ + 16],
       misc[:, M_MB3:M_MB3 + 16], misc[:, M_INVB:M_INVB + 1])
    m_quantc = VDR(True)
    m_last_ev = m_quantc
    W("tensor", "ve", m_quantc)
    for jh in range(2):
        OP("tensor", "matmul", ps_t[:, 0:512],
           misc[:, M_XSROW:M_XSROW + 128],
           misc[:, M_FWS + jh * 512:M_FWS + (jh + 1) * 512],
           start=True, stop=True, inc=(("te", 1),))
        W("vector", "te", mark("te"))
        OP("vector", "tensor_copy", sbig[:, jh * 512:(jh + 1) * 512], ps_t[:, 0:512],
           inc=(("ve", 1),))
        W("tensor", "ve", mark("ve"))
    for jh in range(2):
        OP("tensor", "matmul", ps_t[:, 0:512], sorp[:, :],
           misc[:, M_FB + jh * 512:M_FB + (jh + 1) * 512],
           start=True, stop=True, inc=(("te", 1),))
        W("vector", "te", mark("te"))
        OP("vector", "tensor_copy", bbig[:, jh * 512:(jh + 1) * 512], ps_t[:, 0:512],
           inc=(("ve", 1),))
        if jh == 0:
            W("tensor", "ve", mark("ve"))
    m_sbig = mark("ve")

    m_ffnev = {}
    m_psfree = {}
    prev_out_dma = None
    prev_tr = None
    def emit_quant(t, tr_done_mark):
        OP("scalar", "activation", osc[:, :], xres[:, t * D:(t + 1) * D], AF.Identity,
           misc[:, M_MBQ + t:M_MBQ + t + 1], misc[:, M_RQ + t:M_RQ + t + 1],
           inc=(("ac", 1),))
        W("vector", "ac", mark("ac"))
        if tr_done_mark is not None:
            W("vector", "te", tr_done_mark)
        OP("vector", "tensor_scalar", kf[:, :], osc[:, :], RND, RND, AL.add,
           AL.subtract, inc=(("ve", 1),))
        return mark("ve")

    W("scalar", "ve", m_quantc)
    rnd_marks = {0: emit_quant(0, None)}
    m_trlast = None
    for t in range(NT):
        pst = ps_a if t % 2 == 0 else ps_b
        m_rnd = rnd_marks[t]
        for rnd in range(2):
            pst_ = ps_t if rnd == 0 else ps_t2
            W("tensor", "ve", m_rnd)
            if tr_bank_ev[rnd] is not None:
                W("tensor", "ac", tr_bank_ev[rnd])
            if t == 0 and rnd == 0:
                W("tensor", "ve", m_sbig)
            for j in range(4):
                db = rnd * 4 + j
                OP("tensor", "matmul", pst_[:, j * 128:(j + 1) * 128],
                   kf[:, db * 128:(db + 1) * 128], si128b[:, :],
                   start=(j == 0), stop=(j == 3),
                   inc=((("te", 1),) if j == 3 else ()))
            m_trlast = mark("te")
            W("scalar", "te", m_trlast)
            OP("scalar", "copy", qfA[:, rnd * 512:(rnd + 1) * 512], pst_[:, :],
               inc=(("ac", 1),))
            tr_bank_ev[rnd] = mark("ac")
        prev_tr = mark("ac")
        W("tensor", "ac", prev_tr)
        if t == 0:
            W("tensor", "gp", fw_marks[-1])
        if t >= 2:
            W("tensor", "ve", m_psfree[t - 2])
        for db in range(NDB):
            for jh in range(2):
                last = (db == NDB - 1 and jh == 1)
                OP("tensor", "matmul", pst[:, jh * 512:(jh + 1) * 512],
                   qfA[:, db * 128:(db + 1) * 128],
                   wA[:, db * D + jh * 512: db * D + (jh + 1) * 512],
                   start=(db == 0), stop=(db == NDB - 1),
                   inc=((("te", 1),) if last else ()))
        m_fmm2 = mark("te")
        if t + 1 < NT:
            rnd_marks[t + 1] = emit_quant(t + 1, m_trlast)
        W("vector", "te", m_fmm2)
        OP("vector", "tensor_mul", atn[:, :], pst[:, :], sbig[:, :],
           inc=(("ve", 1),))
        m_psfree[t] = mark("ve")
        VDR()
        OP("vector", "scalar_tensor_tensor", fpk[:, 0:1024], atn[:, :], 1.0,
           bbig[:, :], AL.mult, AL.add, accum_out=misc[:, M_SING:M_SING + 1])
        VDR(True)
        W("scalar", "ve", mark("ve"))
        OP("scalar", "activation", qsil[:, :], fpk[:, 0:1024], AF.Square, 0.0, 1.0,
           accum_out=misc[:, M_SING + 1:M_SING + 2], inc=(("ac", 1),))
        W("vector", "ac", mark("ac"))
        OP("vector", "tensor_scalar_mul", misc[:, M_SING + 2:M_SING + 3],
           misc[:, M_SING:M_SING + 1], 1.0 / D)
        OP("vector", "tensor_scalar_mul", misc[:, M_SING + 3:M_SING + 4],
           misc[:, M_SING + 1:M_SING + 2], 1.0 / D)
        VDR()
        OP("vector", "scalar_tensor_tensor", misc[:, M_SING + 4:M_SING + 5],
           misc[:, M_SING + 2:M_SING + 3], -1.0, misc[:, M_SING + 2:M_SING + 3],
           AL.mult, AL.mult)
        VDR()
        OP("vector", "tensor_add", misc[:, M_SING + 3:M_SING + 4],
           misc[:, M_SING + 3:M_SING + 4], misc[:, M_SING + 4:M_SING + 5])
        mv = VDR(True)
        W("scalar", "ve", mv)
        OP("scalar", "activation", misc[:, M_SING + 5:M_SING + 6],
           misc[:, M_SING + 3:M_SING + 4], AF.Sqrt, 1e-5, 1.0, inc=(("ac", 1),))
        W("vector", "ac", mark("ac"))
        OP("vector", "reciprocal", misc[:, M_SING + 6:M_SING + 7],
           misc[:, M_SING + 5:M_SING + 6])
        VDR()
        OP("vector", "scalar_tensor_tensor", misc[:, M_SING + 7:M_SING + 8],
           misc[:, M_SING + 2:M_SING + 3], -1.0, misc[:, M_SING + 6:M_SING + 7],
           AL.mult, AL.mult)
        VDR()
        mvv = VDR(True)
        W("scalar", "ve", mvv)
        OP("scalar", "activation", atn[:, :], fpk[:, 0:1024], AF.Identity,
           misc[:, M_SING + 7:M_SING + 8], misc[:, M_SING + 6:M_SING + 7],
           inc=(("ac", 1),))
        W("vector", "ac", mark("ac"))
        if prev_out_dma is not None:
            W("vector", "da", prev_out_dma)
        OP("vector", "tensor_add", fpk[:, 1024:2048], xres[:, t * D:(t + 1) * D],
           atn[:, :], inc=(("ve", 1),))
        m_ffnev[t] = mark("ve")
        W("scalar", "ve", m_ffnev[t])
        OP("scalar", "dma_start", dout[t * 128:(t + 1) * 128, :], fpk[:, 1024:2048],
           inc=(("da", 16),))
        prev_out_dma = mark("da")
    W("scalar", "da", ctr["da"])

    if debug:
        OP("scalar", "copy", atn[:, 0:512], tbuf[:, 0:512])
        ADR(True)
        OP("scalar", "dma_start", dbg_gb2 := dbg_tb2[:, :] if False else dbg_tb2[:, :], atn[:, 0:512], inc=(("da", 16),))
        OP("scalar", "dma_start", dbg_misc[:, :], misc[:, :], inc=(("da", 16),))
        OP("scalar", "dma_start", dbg_gb[:, :], gb[:, :], inc=(("da", 16),))
        for t in range(NT):
            W("scalar", "da", ctr["da"])
            OP("scalar", "copy", atn[:, :], xres[:, t * D:(t + 1) * D])
            ADR(True)
            OP("scalar", "dma_start", dbg_xres[:, t * D:(t + 1) * D], atn[:, :],
               inc=(("da", 16),))
            W("scalar", "da", ctr["da"])
            OP("scalar", "copy", atn[:, :], ktm[:, t * D:(t + 1) * D])
            ADR(True)
            OP("scalar", "dma_start", dbg_ktm[:, t * D:(t + 1) * D], atn[:, :],
               inc=(("da", 16),))
        W("scalar", "da", ctr["da"])

    if trunc is not None:
        plan[:] = plan[:ckpt[trunc]]
        tctr = {k: 0 for k in ctr}
        for it in plan:
            if it[1] == "op":
                for s, n in it[5]:
                    tctr[s] += n
        for k in ctr:
            ctr[k] = tctr[k]
        plan.append(("sync", "w", "ds", ctr["ds"], None, None))
        plan.append(("scalar", "w", "gp", ctr["gp"], None, None))
        plan.append(("scalar", "w", "ve", ctr["ve"], None, None))
        plan.append(("scalar", "w", "ac", ctr["ac"], None, None))
        plan.append(("scalar", "w", "te", ctr["te"], None, None))
        plan.append(("scalar", "w", "cc", ctr["cc"], None, None))
        plan.append(("scalar", "w", "da", ctr["da"], None, None))
        plan.append(("scalar", "op", "dma_start",
                     (dout[0:128, :], misc[:, 0:1024]), {}, [("da", 16)]))
        ctr["da"] += 16
        plan.append(("scalar", "w", "da", ctr["da"], None, None))
        for e in ("tensor", "vector", "gpsimd"):
            plan.append((e, "op", "nop", (), {}, []))

    def runner(name):
        def f(eng):
            for it in plan:
                if it[0] != name:
                    continue
                if it[1] == "w":
                    eng.wait_ge(sems[it[2]], it[3])
                else:
                    _, _, meth, a, k, incs = it
                    ins = getattr(eng, meth)(*a, **k)
                    for s, n in incs:
                        ins = ins.then_inc(sems[s], n)
        return f

    with nc.Block() as blk:
        blk.sync(runner("sync"))
        blk.tensor(runner("tensor"))
        blk.vector(runner("vector"))
        blk.scalar(runner("scalar"))
        blk.gpsimd(runner("gpsimd"))
    return nc


_cache = {}


def kernel(L0_attn_x, L0_attn_Wq, L0_attn_Wk, L0_attn_Wv, L0_attn_Wo, L0_attn_Wf,
           L0_attn_delta, L0_attn_norm_weight, L0_ffn_weight_int8,
           L0_ffn_weight_scale, L0_ffn_bias):
    x = np.ascontiguousarray(np.asarray(L0_attn_x, dtype=np.float32).reshape(16384, 1024))
    wqt = np.ascontiguousarray(np.asarray(L0_attn_Wq, dtype=np.float32).T)
    wkt = np.ascontiguousarray(np.asarray(L0_attn_Wk, dtype=np.float32).T)
    wvt = np.ascontiguousarray(np.asarray(L0_attn_Wv, dtype=np.float32).T)
    wot = np.ascontiguousarray(np.asarray(L0_attn_Wo, dtype=np.float32).T)
    wft = np.ascontiguousarray(np.asarray(L0_attn_Wf, dtype=np.float32).T)
    delta = np.ascontiguousarray(np.asarray(L0_attn_delta, dtype=np.float32).reshape(16, 1))
    nw = np.ascontiguousarray(np.asarray(L0_attn_norm_weight, dtype=np.float32).reshape(8, 128).T)
    fw = np.ascontiguousarray(np.asarray(L0_ffn_weight_int8).T)
    fws = np.ascontiguousarray(np.asarray(L0_ffn_weight_scale, dtype=np.float32).reshape(1, 1024))
    fb = np.ascontiguousarray(np.asarray(L0_ffn_bias, dtype=np.float32).reshape(1, 1024))

    if "nc" not in _cache:
        _cache["nc"] = build_nc()
    nc = _cache["nc"]
    in_maps = []
    for s in range(NCORE):
        sm = np.zeros((16, 8), dtype=np.float32)
        sm[:, :s] = 1.0
        in_maps.append({
            "x": np.ascontiguousarray(x[s * TOK:(s + 1) * TOK, :]),
            "wqt": wqt, "wkt": wkt, "wvt": wvt, "wot": wot, "wft": wft,
            "delta": delta, "nw": nw, "fw": fw, "fws": fws, "fb": fb,
            "segmask": sm,
        })
    res = run_bass_kernel_spmd(nc, in_maps, core_ids=list(range(NCORE)))
    out = np.concatenate([np.asarray(r["out"]) for r in res.results], axis=0)
    return out.reshape(1, 16384, 1024).astype(np.float32)
